# version 01
# baseline (speedup 1.0000x reference)
import numpy as np
import concourse.bass as bass
import concourse.mybir as mybir

F32 = mybir.dt.float32
BF16 = mybir.dt.bfloat16
AF = mybir.ActivationFunctionType
ALU = mybir.AluOpType
AX = mybir.AxisListType

L, D, DI, NH, HD, NST = 4096, 2048, 4096, 64, 64, 64
TT, S = 512, 128
NTILE, NCH = L // TT, TT // S
DIP = 8384
NK = D // 128
NKI = DI // 128
ENGS = ("sync", "vector", "scalar", "tensor", "gpsimd")


def sl(off, n):
    return slice(off, off + n)


class Sem:
    __slots__ = ("h", "val")

    def __init__(self, nc):
        self.h = nc.alloc_semaphore()
        self.val = 0


class KB:
    def __init__(self, debug=()):
        nc = self.nc = bass.Bass("TRN2", target_bir_lowering=False)
        self.streams = {e: [] for e in ENGS}
        self.debug = dict(debug)
        self.dbg_t = {}
        self.dbg_done = set()

        din = lambda n, sh: nc.dram_tensor(n, sh, F32, kind="ExternalInput").ap()
        self.u = din("u", [L, D])
        self.w_in = din("w_in", [D, DIP])
        self.convw = din("convw", [128, 128])
        self.convb = din("convb", [128, 32])
        self.A_log = din("A_log", [NH, 1])
        self.dt_bias = din("dt_bias", [NH, 1])
        self.D_param = din("D_param", [1, NH])
        self.norm_scale = din("norm_scale", [128, 32])
        self.w_out = din("w_out", [DI, D])
        self.w_base = din("w_base", [D, D])
        self.b_base = din("b_base", [1, D])
        self.w_small = din("w_small", [D, 118])
        self.b_down = din("b_down", [112, D])
        self.brow = din("brow", [1, 118])
        self.eye128 = din("eye128", [128, 128])
        self.mask01 = din("mask01", [128, 128])
        self.out = nc.dram_tensor("out", [L, D], F32, kind="ExternalOutput").ap()
        for name, (shape, dt_) in self.debug.items():
            self.dbg_t[name] = nc.dram_tensor("dbg_" + name, shape, dt_, kind="ExternalOutput").ap()

        dint = lambda n, sh, dt_: nc.dram_tensor(n, sh, dt_, kind="Internal").ap()
        self.wsc_in = dint("wsc_in", [NK, 128, DIP], BF16)
        self.wsc_out = dint("wsc_out", [NKI, 128, D], BF16)
        self.wsc_base = dint("wsc_base", [NK, 128, D], BF16)
        self.wsc_diag = dint("wsc_diag", [32, 4, 128, 128], BF16)
        self.zsc = dint("zsc", [NCH, 128, DI], BF16)
        self.csflat = dint("csflat", [2, NH, S], F32)
        self.rowsc = dint("rowsc", [1, D], BF16)

        DTSZ = {F32: 4, BF16: 2}
        ARENA = 206848
        self.arena = nc.alloc_sbuf_tensor("arena", [128, ARENA], mybir.dt.uint8)
        self._abase = nc.lookup_mloc(self.arena).addr
        self._cursor = 0
        self._mstart = {}
        self._table = []

        def A(n, sh, dt_, overlay=None):
            nbytes = sh[-1] * DTSZ[dt_]
            for d in sh[1:-1]:
                nbytes *= d
            nbytes = (nbytes + 31) // 32 * 32
            if overlay is not None:
                start = overlay
            else:
                start = self._cursor
                self._cursor += nbytes
            assert start + nbytes <= ARENA, f"SBUF arena overflow at {n}: {start + nbytes}"
            self._table.append((n, start, nbytes, overlay is not None))
            h = nc.virtual_sbuf_tensor(n, sh, dt_)
            nc._tensor(h.name, h.shape, h.dtype, type="SB")
            mloc = nc.lookup_mloc(h)
            mloc.allocated = True
            mloc.addr = self._abase + start
            ap = h.ap()
            self._mstart[n] = start
            return ap

        self.eye128_b = A("eye128_b", [128, 128], BF16)
        self.eye64_f = A("eye64_f", [64, 64], F32)
        self.mask01_b = A("mask01_b", [128, 128], BF16)
        self.ns_cols = A("ns_cols", [128, NKI], F32)
        self.cb_cols = A("cb_cols", [128, 32], F32)
        self.negdtb = A("negdtb", [NH, 1], F32)
        self.expA = A("expA", [NH, 1], F32)
        self.drep = A("drep", [128, NH], BF16)
        self.wsmall_sb = A("wsmall_sb", [128, NK * 118], BF16)
        self.bdown_sb = A("bdown_sb", [113, D], BF16)
        self.brow_sb = A("brow_sb", [1, 118], BF16)
        self.ones_row = A("ones_row", [1, 128], BF16)
        self.cwT = A("cwT", [128, 128], F32)

        self.vt = A("vt", [128, NCH * DI], BF16)
        self.xslot = A("xslot", [128, NCH * D], F32, overlay=self._mstart["vt"])
        self.slotA = A("slotA", [128, 8192], BF16)
        self.slotB = A("slotB", [128, 8192], BF16)
        self.zbuf = None
        self.xrr = A("xrr", [128, 6 * 515], BF16)
        self.xcr = A("xcr", [128, 4 * 512], BF16)
        self.zbuf = A("zbuf", [128, DI], BF16, overlay=self._mstart["xrr"])
        self.darg = A("darg", [128, 2048], BF16)
        self.datt = A("datt", [128, 2048], BF16)
        self.csb = A("csb", [128, 2048], F32)
        self.ygt = A("ygt", [128, DI], BF16)
        self.prep_b = self.ygt[:, 0:2048]
        self.ntok = self.ygt[:, 2048:4096]
        self.rv = A("rv", [128, 1024], BF16)
        self.vd = A("vd", [128, 1024], BF16)
        self.y_tmp = A("y_tmp", [128, 1024], F32)
        self.scrF = A("scrF", [128, DI], F32)
        self.y_sb = self.scrF[:, 0:DI]
        self.lns = self.scrF[:, 0:D]
        self.x2_sb = self.scrF[:, D:2 * D]
        self.prep_f = self.scrF[:, 0:DI]
        self.h_sb = A("h_sb", [NST, DI], F32)
        self.h_bf = A("h_bf", [NST, DI], BF16)

        self.wbuf = A("wbuf", [128, 2 * NK * 256], BF16)
        self.zwbuf = A("zwbuf", [128, 2 * 2 * 512], BF16)
        self.owbuf = A("owbuf", [128, 2 * 2 * 512], BF16)
        self.bwbuf = A("bwbuf", [128, 2 * 2 * 512], BF16)
        self.dgbuf = A("dgbuf", [128, 2 * 4 * 128], BF16)

        self.smt = A("smt", [128, S], BF16)
        self.bm_b = A("bm_b", [64, TT], BF16)
        self.cm_b = A("cm_b", [64, TT], BF16)
        self.rh = A("rh", [NH, S], BF16)
        self.rt = A("rt", [128, NH], BF16)
        self.kc_sb = A("kc_sb", [128, NST], BF16)
        self.hist = A("hist", [128, 32 * 3], BF16)
        self.stackt = A("stackt", [128, 113], BF16)
        self.stackT = A("stackT", [113, TT], BF16)
        self.ffn_small = A("ffn_small", [128, 118], BF16)
        self.csh = A("csh", [NH, S], F32)
        self.cst = A("cst", [128, NH], F32)
        self.et = A("et", [128, NH], F32)
        self.la = A("la", [NH, TT], F32)
        self.dtile = A("dtile", [NST, NH], F32)
        self.dcol = A("dcol", [NH, 1], F32)
        self.gates = A("gates", [128, 6], F32)
        self.eg = A("eg", [128, 6], F32)
        self.gm = A("gm", [128, 6], F32)
        self.wpe = A("wpe", [128, 6], F32)
        cols = lambda n: A(n, [128, 1], F32)
        self.sum_col, self.ssq_col, self.ms_col = cols("sum_col"), cols("ssq_col"), cols("ms_col")
        self.rstd_col, self.nmr_col = cols("rstd_col"), cols("nmr_col")
        self.ss_col, self.rstd2 = cols("ss_col"), cols("rstd2")
        self.m1c, self.m2c, self.sumg, self.rsg = cols("m1c"), cols("m2c"), cols("sumg"), cols("rsg")

        Pp = lambda n, sh, dt_: nc.alloc_psum_tensor(n, sh, dt_).ap()
        self.psA = Pp("psA", [128, 2048], F32)
        self.psB = Pp("psB", [128, 1024], F32)
        self.ms_ps = Pp("ms_ps", [128, 512], F32)
        self.tr_ps = Pp("tr_ps", [128, 512], BF16)

        self.S_ = {n: Sem(nc) for n in ("sW", "sG", "sV", "sA", "sT")}
        self.DMA_SEMS = {"sW", "sG"}
        self._conv_evac = {}

    def normed(self):
        return self.slotA

    def n2f_slice(self, k, tsub):
        return self.slotA[:, sl(k * 512 + tsub * 128, 128)]

    def ygf_slice(self, k, tsub):
        base = k * 512 + tsub * 128
        return (self.slotA if k < 16 else self.slotB)[:, sl(base - (8192 if k >= 16 else 0), 128)]

    def emit(self, eng, fn):
        self.streams[eng].append(fn)

    def sem(self, name):
        if name not in self.S_:
            self.S_[name] = Sem(self.nc)
        return self.S_[name]

    def wait(self, eng, semname, val):
        s = self.sem(semname)
        self.emit(eng, lambda E, h=s.h, v=val: E.wait_ge(h, v))

    def run(self, eng, fn, inc=None, n=1):
        if inc is not None:
            s = self.sem(inc)
            s.val += n
            self.emit(eng, lambda E, fn=fn, h=s.h, n=n: fn(E).then_inc(h, n))
            return s.val
        self.emit(eng, lambda E, fn=fn: fn(E))
        return None

    def dma(self, eng, out, in_, slow=False, sem=None):
        inc = sem or {"sync": "sW", "gpsimd": "sG"}[eng]
        self.DMA_SEMS.add(inc)
        return self.run(eng, lambda E, o=out, i=in_, s=slow: E.dma_start(
            out=o, in_=i, allow_slow_non_contiguous=s), inc, 16)

    def v(self, name):
        return self.sem(name).val

    def selfsync(self, eng):
        em = {"vector": "sV", "scalar": "sA", "tensor": "sT"}[eng]
        self.wait(eng, em, self.S_[em].val)

    def barrier(self, engines=("vector", "scalar", "tensor"), dma=("sW", "sG")):
        em = {"vector": "sV", "scalar": "sA", "tensor": "sT"}
        for e in engines:
            self.run(e, lambda E: E.drain(), em[e], 1)
        dma = tuple(self.DMA_SEMS)
        vals = {n: self.v(n) for n in list(em.values()) + list(dma)}
        targets = [em[e] for e in engines] + list(dma)
        for e in engines + ("sync", "gpsimd"):
            for sn in targets:
                self.wait(e, sn, vals[sn])

    def tap(self, name, src_ap):
        if name in self.dbg_t and name not in self.dbg_done:
            self.dbg_done.add(name)
            self.barrier()
            self.dma("gpsimd", self.dbg_t[name], src_ap, sem="sSO")
            self.barrier()

    def finalize(self):
        for sn in sorted(self.DMA_SEMS):
            self.wait("gpsimd", sn, self.v(sn))
        nc = self.nc
        with nc.Block() as block:
            for eng in ENGS:
                stream = self.streams[eng]
                dec = getattr(block, eng)

                def make(stream):
                    def body(E):
                        for fn in stream:
                            fn(E)
                    return body
                dec(make(stream))
        return nc



def emit_prep(kb: "KB"):
    p = kb
    st = p.scrF
    p.dma("gpsimd", st[:, 0:128], p.eye128)
    p.dma("gpsimd", p.eye64_f, p.eye128[0:64, 0:64])
    p.dma("gpsimd", st[:, 128:256], p.mask01)
    p.dma("gpsimd", p.ns_cols, p.norm_scale)
    p.dma("gpsimd", p.cb_cols, p.convb)
    p.dma("gpsimd", st[0:64, 2208:2209], p.dt_bias)
    p.dma("gpsimd", st[0:64, 2209:2210], p.A_log)
    p.dma("gpsimd", st[:, 256:320],
          p.D_param.rearrange("a (o h) -> a o h", o=1).broadcast_to([1, 128, NH]))
    p.dma("gpsimd", st[:, 320:320 + 16 * 118].rearrange("q (k c) -> q k c", k=16),
          p.w_small.rearrange("(k q) c -> q k c", q=128))
    p.dma("gpsimd", p.cwT, p.convw)
    gv = p.v("sG")
    p.wait("vector", "sG", gv)
    p.run("vector", lambda E: E.tensor_copy(p.eye128_b[:], st[:, 0:128]), "sV")
    p.run("vector", lambda E: E.tensor_copy(p.mask01_b[:], st[:, 128:256]), "sV")
    p.run("vector", lambda E: E.tensor_copy(p.drep[:], st[:, 256:320]), "sV")
    p.run("vector", lambda E: E.tensor_copy(p.wsmall_sb[:], st[:, 320:320 + 16 * 118]), "sV")
    p.run("vector", lambda E: E.tensor_scalar(out=p.negdtb[:], in0=st[0:64, 2208:2209],
                                              scalar1=-1.0, scalar2=None, op0=ALU.mult), "sV")
    p.wait("scalar", "sG", gv)
    expav = p.run("scalar", lambda E: E.activation(out=p.expA[:], in_=st[0:64, 2209:2210],
                                           func=AF.Exp, bias=0.0, scale=1.0), "sA")
    p.run("vector", lambda E: E.memset(p.ones_row[:], 1.0), "sV")
    p.run("vector", lambda E: E.memset(p.stackt[:, 112:113], 1.0), "sV")
    p.run("vector", lambda E: E.memset(p.h_sb[:], 0.0), "sV")
    p.run("vector", lambda E: E.memset(p.h_bf[:], 0.0), "sV")
    vfence = p.run("vector", lambda E: E.memset(p.hist[:], 0.0), "sV")

    p.wait("gpsimd", "sV", vfence)
    p.dma("gpsimd", st[0:112, 0:D], p.b_down)
    p.dma("gpsimd", p.csb[0:1, 0:D], p.b_base)
    p.dma("gpsimd", p.y_tmp[0:1, 0:118], p.brow)
    p.wait("vector", "sG", p.v("sG"))
    p.run("vector", lambda E: E.tensor_copy(p.bdown_sb[0:112, :], st[0:112, 0:D]), "sV")
    bb = p.run("vector", lambda E: E.tensor_copy(p.zbuf[0:1, 0:D], p.csb[0:1, 0:D]), "sV")
    vfence2 = p.run("vector", lambda E: E.tensor_copy(p.brow_sb[:], p.y_tmp[0:1, 0:118]), "sV")
    p.wait("gpsimd", "sV", bb)
    rw = p.dma("gpsimd", p.rowsc, p.zbuf[0:1, 0:D])
    p.wait("gpsimd", "sG", rw)
    p.dma("gpsimd", p.bdown_sb[112:113, :], p.rowsc)
    p.wait("sync", "sV", vfence2)
    p.wait("gpsimd", "sV", vfence2)

    p.wait("vector", "sV", p.v("sV"))
    for fc in range(32):
        p.wait("vector", "sW", p.v("sW"))
        for j in range(4):
            p.run("vector", lambda E, fc=fc, j=j: E.tensor_tensor(
                out=p.prep_b[:, sl(j * 128, 128)], in0=p.eye128_b[:],
                in1=p.cwT[:, fc * 4 + j:fc * 4 + j + 1].broadcast_to([128, 128]),
                op=ALU.mult), "sV")
        vv = p.v("sV")
        p.wait("sync", "sV", vv)
        p.dma("sync", p.wsc_diag[fc].rearrange("j q c -> q j c"), p.prep_b[:, 0:512].rearrange("q (j c) -> q j c", j=4))

    pieces = [(i * 2048, 2048) for i in range(4)] + [(8192, 192)]
    for k in range(NK):
        for (off, w) in pieces:
            p.dma("sync", p.prep_f[:, 0:w], p.w_in[sl(k * 128, 128), sl(off, w)])
            p.wait("vector", "sW", p.v("sW"))
            p.run("vector", lambda E, w=w: E.tensor_copy(p.prep_b[:, 0:w], p.prep_f[:, 0:w]), "sV")
            p.wait("sync", "sV", p.v("sV"))
            p.dma("sync", p.wsc_in[k][:, sl(off, w)], p.prep_b[:, 0:w])

    for k in range(NKI):
        p.dma("sync", p.prep_f[:, 0:D], p.w_out[sl(k * 128, 128), :])
        p.wait("vector", "sW", p.v("sW"))
        p.run("vector", lambda E, k=k: E.tensor_scalar(
            out=p.prep_b[:, 0:D], in0=p.prep_f[:, 0:D],
            scalar1=p.ns_cols[:, k:k + 1], scalar2=None, op0=ALU.mult), "sV")
        p.wait("sync", "sV", p.v("sV"))
        p.dma("sync", p.wsc_out[k][:, :], p.prep_b[:, 0:D])

    for k in range(NK):
        p.dma("sync", p.prep_f[:, 0:D], p.w_base[sl(k * 128, 128), :])
        p.wait("vector", "sW", p.v("sW"))
        p.run("vector", lambda E: E.tensor_copy(p.prep_b[:, 0:D], p.prep_f[:, 0:D]), "sV")
        p.wait("sync", "sV", p.v("sV"))
        p.dma("sync", p.wsc_base[k][:, :], p.prep_b[:, 0:D])

    p.barrier()
    kb.tap("wsc_probe", kb.wsc_in[0][:, 0:1024])


def emit_ln(kb: "KB", t, dst_slot, tap_prefix):
    p = kb
    for tsub in range(NCH):
        xs = p.xslot[:, sl(tsub * D, D)]
        p.selfsync("vector")
        p.selfsync("scalar")
        vs = p.run("vector", lambda E, xs=xs: E.tensor_reduce(
            out=p.sum_col[:], in_=xs, axis=AX.X, op=ALU.add), "sV")
        va = p.run("scalar", lambda E, xs=xs: E.activation(
            out=p.lns[:, 0:D], in_=xs, func=AF.Square, bias=0.0, scale=1.0,
            accum_out=p.ssq_col[:]), "sA")
        p.wait("vector", "sV", vs)
        vms = p.run("vector", lambda E: E.tensor_tensor(
            out=p.ms_col[:], in0=p.sum_col[:], in1=p.sum_col[:], op=ALU.mult), "sV")
        p.wait("vector", "sV", vms)
        p.wait("vector", "sA", va)
        vs2 = p.run("vector", lambda E: E.scalar_tensor_tensor(
            out=p.ms_col[:], in0=p.ms_col[:], scalar=-1.0 / D, in1=p.ssq_col[:],
            op0=ALU.mult, op1=ALU.add), "sV")
        p.wait("vector", "sV", vs2)
        vs2b = p.run("vector", lambda E: E.tensor_scalar(
            out=p.ms_col[:], in0=p.ms_col[:], scalar1=1.0 / D, scalar2=1e-5,
            op0=ALU.mult, op1=ALU.add), "sV")
        p.wait("vector", "sV", vs2b)
        vs2c = p.run("vector", lambda E: E.reciprocal(p.nmr_col[:], p.ms_col[:]), "sV")
        p.wait("scalar", "sV", vs2c)
        va2 = p.run("scalar", lambda E: E.activation(
            out=p.rstd_col[:], in_=p.nmr_col[:], func=AF.Sqrt, bias=0.0, scale=1.0), "sA")
        p.wait("vector", "sA", va2)
        vs3 = p.run("vector", lambda E: E.scalar_tensor_tensor(
            out=p.nmr_col[:], in0=p.sum_col[:], scalar=-1.0 / D, in1=p.rstd_col[:],
            op0=ALU.mult, op1=ALU.mult), "sV")
        p.wait("scalar", "sV", vs3)
        va3 = p.run("scalar", lambda E, xs=xs: E.activation(
            out=p.ntok[:], in_=xs,
            func=AF.Identity, bias=p.nmr_col[:], scale=p.rstd_col[:]), "sA")
        p.wait("tensor", "sA", va3)
        for kg in range(4):
            for k4 in range(4):
                k = kg * 4 + k4
                p.run("tensor", lambda E, k=k, k4=k4: E.transpose(
                    p.tr_ps[:, sl(k4 * 128, 128)], p.ntok[:, sl(k * 128, 128)],
                    p.eye128_b[:]), None)
            vt_ = p.run("tensor", lambda E: E.drain(), "sT")
            p.wait("scalar", "sT", vt_)
            va4 = p.run("scalar", lambda E, kg=kg, tsub=tsub: E.activation(
                out=dst_slot.rearrange("q (k c) -> q k c", k=16)[:, sl(kg * 4, 4), sl(tsub * 128, 128)],
                in_=p.tr_ps[:].rearrange("q (k c) -> q k c", k=4),
                func=AF.Copy, bias=0.0, scale=1.0), "sA")
            p.wait("tensor", "sA", va4)


def emit_inproj_f(kb: "KB", t):
    p = kb
    conv_done = {}
    vtr_done = {}
    evac_done = {}
    psum_evac = {}
    wbuf_free = {}
    dg_free = {}

    def stream_group(g):
        par = g % 2
        w = 256 if g < 16 else 192
        cols = sl(4096 + g * 256, w)
        if par in wbuf_free:
            p.wait("sync", "sT", wbuf_free[par])
        sname = "wPA" if par == 0 else "wPB"
        p.dma("sync", p.wbuf[:, sl(par * 4096, 16 * w)].rearrange("q (k c) -> q k c", k=16),
              p.wsc_in[:, :, cols].rearrange("k q c -> q k c"), sem=sname)
        return (sname, p.v(sname))

    def stream_diag(fc):
        par = fc % 2
        if par in dg_free:
            p.wait("sync", "sT", dg_free[par])
        sname = "dgA" if par == 0 else "dgB"
        p.dma("sync", p.dgbuf[:, sl(par * 512, 512)].rearrange("q (j c) -> q j c", j=4),
              p.wsc_diag[fc].rearrange("j q c -> q j c"), sem=sname)
        return (sname, p.v(sname))

    wsem = {0: stream_group(0), 1: stream_group(1)}
    dgsem = {0: stream_diag(0), 1: stream_diag(1)}

    for g in range(17):
        par = g % 2
        p.wait("tensor", *wsem[g])
        if g < 16:
            for fi in range(2):
                fc = 2 * g + fi
                psv = p.psA[:, sl(fi * 512, 512)]
                if fi in psum_evac:
                    p.wait("tensor", "sA", psum_evac[fi])
                for k in range(NK):
                    base = par * 4096 + k * 256 + fi * 128
                    p.run("tensor", lambda E, base=base, psv=psv, k=k: E.matmul(
                        psv, p.wbuf[:, sl(base, 128)], p.normed()[:, sl(k * 512, 512)],
                        start=(k == 0), stop=(k == NK - 1)), None)
                mmv = p.run("tensor", lambda E: E.drain(), "sT")
                if fi == 1:
                    wbuf_free[par] = mmv
                    if g + 2 <= 16:
                        wsem[g + 2] = stream_group(g + 2)
                if fc >= 6:
                    p.wait("scalar", "sT", conv_done[fc - 6])
                p.wait("scalar", "sT", mmv)
                ev = p.run("scalar", lambda E, fc=fc, psv=psv: E.activation(
                    out=p.xrr[:, sl((fc % 6) * 515 + 3, 512)], in_=psv,
                    func=AF.Copy, bias=0.0, scale=1.0), "sA")
                evac_done[fc] = ev
                psum_evac[fi] = ev
                if fc == 0:
                    p.wait("gpsimd", "sA", ev)
                    kb.tap("wbuf0", p.wbuf[:, 0:4096])
        else:
            p.wait("tensor", "sA", max(list(psum_evac.values()) + list(kb._conv_evac.values())))
            for j in range(3):
                psv = p.psA[0:64, sl(j * 512, 512)]
                coff = j * 64
                for k in range(NK):
                    base = par * 4096 + k * 192 + coff
                    p.run("tensor", lambda E, base=base, psv=psv, k=k: E.matmul(
                        psv, p.wbuf[:, sl(base, 64)], p.normed()[:, sl(k * 512, 512)],
                        start=(k == 0), stop=(k == NK - 1)), None)
            bcv = p.run("tensor", lambda E: E.drain(), "sT")
            wbuf_free[par] = bcv
            p.wait("vector", "sT", bcv)
            p.wait("scalar", "sT", bcv)
            for j, dst in [(0, p.bm_b), (1, p.cm_b)]:
                psv = p.psA[0:64, sl(j * 512, 512)]
                v1 = p.run("vector", lambda E, psv=psv: E.tensor_scalar(
                    out=p.lns[0:64, 0:512], in0=psv, scalar1=0.0, scalar2=None,
                    op0=ALU.min), "sV")
                p.wait("scalar", "sV", v1)
                a1 = p.run("scalar", lambda E: E.activation(
                    out=p.lns[0:64, 512:1024], in_=p.lns[0:64, 0:512],
                    func=AF.Exp, bias=0.0, scale=1.0), "sA")
                p.wait("vector", "sA", a1)
                p.run("vector", lambda E, psv=psv, dst=dst: E.scalar_tensor_tensor(
                    out=dst[:, 0:512], in0=psv, scalar=1.0, in1=p.lns[0:64, 512:1024],
                    op0=ALU.add, op1=ALU.max), "sV")
            psv = p.psA[0:64, sl(2 * 512, 512)]
            sgv = p.run("scalar", lambda E, psv=psv: E.activation(
                out=p.lns[0:64, 1024:1536], in_=psv,
                func=AF.Sigmoid, bias=p.negdtb[:], scale=-1.0), "sA")
            p.wait("scalar", "sA", sgv)
            a4 = p.run("scalar", lambda E: E.activation(
                out=p.lns[0:64, 1536:2048], in_=p.lns[0:64, 1024:1536],
                func=AF.Ln, bias=0.0, scale=1.0), "sA")
            p.wait("vector", "sA", a4)
            p.run("vector", lambda E: E.tensor_scalar(
                out=p.la[:, 0:512], in0=p.lns[0:64, 1536:2048],
                scalar1=p.expA[:], scalar2=None, op0=ALU.mult), "sV")

        if g < 16:
            for fi in range(2):
                fc = 2 * g + fi
                par_d = fc % 2
                p.wait("vector", "sA", evac_done[fc])
                hv = p.run("vector", lambda E, fc=fc: E.tensor_copy(
                    p.xrr[:, sl((fc % 6) * 515, 3)], p.hist[:, sl(fc * 3, 3)]), "sV")
                p.wait("tensor", "sV", hv)
                p.wait("tensor", *dgsem[fc])
                cps = p.psA[:, sl(1024 + par_d * 512, 512)]
                if fc >= 2:
                    p.wait("tensor", "sA", kb._conv_evac.get(par_d, 0))
                for j in range(4):
                    p.run("tensor", lambda E, j=j, par_d=par_d, fc=fc, cps=cps: E.matmul(
                        cps, p.dgbuf[:, sl(par_d * 512 + j * 128, 128)],
                        p.xrr[:, (fc % 6) * 515 + j:(fc % 6) * 515 + j + 512],
                        start=(j == 0), stop=(j == 3)), None)
                cv = p.run("tensor", lambda E: E.drain(), "sT")
                conv_done[fc] = cv
                dg_free[par_d] = cv
                if fc + 2 < 32:
                    dgsem[fc + 2] = stream_diag(fc + 2)
                p.wait("vector", "sV", hv)
                p.run("vector", lambda E, fc=fc: E.tensor_copy(
                    p.hist[:, sl(fc * 3, 3)], p.xrr[:, sl((fc % 6) * 515 + 3 + 509, 3)]), "sV")
                if fc >= 4:
                    p.wait("scalar", "sT", vtr_done[fc - 4])
                p.wait("scalar", "sT", cv)
                xe = p.run("scalar", lambda E, fc=fc, cps=cps: E.activation(
                    out=p.xcr[:, sl((fc % 4) * 512, 512)], in_=cps,
                    func=AF.Silu, bias=p.cb_cols[:, fc:fc + 1], scale=1.0), "sA")
                kb._conv_evac[par_d] = xe
                p.wait("tensor", "sA", xe)
                for c in range(NCH):
                    p.run("tensor", lambda E, fc=fc, c=c: E.transpose(
                        p.tr_ps[:, sl(c * 128, 128)],
                        p.xcr[:, sl((fc % 4) * 512 + c * 128, 128)], p.eye128_b[:]), None)
                tv = p.run("tensor", lambda E: E.drain(), "sT")
                vtr_done[fc] = tv
                p.wait("scalar", "sT", tv)
                ve = p.run("scalar", lambda E, fc=fc: E.activation(
                    out=p.vt[:].rearrange("q (c f) -> q c f", c=NCH)[:, :, sl(fc * 128, 128)],
                    in_=p.tr_ps[:].rearrange("q (c f) -> q c f", c=NCH),
                    func=AF.Copy, bias=0.0, scale=1.0), "sA")
                p.wait("tensor", "sA", ve)


def emit_inproj_z(kb: "KB", t):
    p = kb
    zfree = {}
    evs = {}

    def stream(zc, kg):
        par = (zc * 8 + kg) % 2
        if par in zfree:
            p.wait("sync", "sT", zfree[par])
        sname = "zwA" if par == 0 else "zwB"
        p.dma("sync", p.zwbuf[:, sl(par * 1024, 1024)].rearrange("q (k c) -> q k c", k=2),
              p.wsc_in[sl(kg * 2, 2), :, sl(zc * 512, 512)].rearrange("k q c -> q k c"), sem=sname)
        return (sname, p.v(sname))

    sems = {(0, 0): stream(0, 0), (0, 1): stream(0, 1)}
    for zc in range(8):
        for kg in range(8):
            par = (zc * 8 + kg) % 2
            p.wait("tensor", *sems[(zc, kg)])
            nxt = (zc, kg + 2) if kg + 2 < 8 else (zc + 1, kg - 6)
            for tsub in range(NCH):
                psv = p.psA[:, sl(tsub * 512, 512)]
                if kg == 0 and tsub in evs:
                    p.wait("tensor", "sA", evs[tsub])
                for k4 in range(2):
                    k = kg * 2 + k4
                    p.run("tensor", lambda E, k=k, k4=k4, kg=kg, par=par, tsub=tsub, psv=psv: E.matmul(
                        psv, p.normed()[:, sl(k * 512 + tsub * 128, 128)],
                        p.zwbuf[:, sl(par * 1024 + k4 * 512, 512)],
                        start=(kg == 0 and k4 == 0), stop=(kg == 7 and k4 == 1)), None)
            mv = p.run("tensor", lambda E: E.drain(), "sT")
            zfree[par] = mv
            if nxt[0] < 8:
                sems[nxt] = stream(*nxt)
        p.wait("scalar", "sT", p.v("sT"))
        p.wait("scalar", "sSO", p.v("sSO"))
        for tsub in range(NCH):
            evs[tsub] = p.run("scalar", lambda E, tsub=tsub: E.activation(
                out=p.prep_b[:, sl(tsub * 512, 512)], in_=p.psA[:, sl(tsub * 512, 512)],
                func=AF.Silu, bias=0.0, scale=1.0), "sA")
        p.wait("gpsimd", "sA", p.v("sA"))
        p.dma("gpsimd", p.zsc[:, :, sl(zc * 512, 512)].rearrange("c q f -> q c f"),
              p.prep_b[:, 0:2048].rearrange("q (c f) -> q c f", c=4), sem="sSO")


def emit_ssd_chunk(kb: "KB", t, c):
    p = kb
    la_sl = p.la[:, sl(c * S, S)]
    csh_sl = p.csh[:, 0:S]
    cst_sl = p.cst[:, 0:NH]
    et_sl = p.et[:, 0:NH]
    rt_sl = p.rt[:, 0:NH]
    smt_sl = p.smt[:, 0:S]
    kcT = p.bm_b[:, sl(c * S, S)]
    qcT = p.cm_b[:, sl(c * S, S)]
    vtc = p.vt[:, sl(c * DI, DI)]

    zv = p.dma("gpsimd", p.zbuf[:, 0:DI], p.zsc[c], sem="sZB")

    sv = p.run("vector", lambda E: E.tensor_tensor_scan(
        out=csh_sl, data0=la_sl, data1=la_sl, initial=0.0,
        op0=ALU.add, op1=ALU.bypass), "sV")
    p.wait("gpsimd", "sV", sv)
    flv = p.dma("gpsimd", p.csflat[0], csh_sl, sem="sFL")
    p.wait("tensor", "sV", sv)
    trv = p.run("tensor", lambda E: E.transpose(
        p.ms_ps[:, 0:64], csh_sl, p.eye64_f[:]), "sT")
    p.wait("vector", "sT", trv)
    cstv = p.run("vector", lambda E: E.tensor_copy(cst_sl, p.ms_ps[:, 0:64]), "sV")
    p.wait("scalar", "sV", cstv)
    etv = p.run("scalar", lambda E: E.activation(
        out=et_sl, in_=cst_sl, func=AF.Exp, bias=0.0, scale=1.0), "sA")
    totcol = p.csh[:, S - 1:S]
    rhv = p.run("scalar", lambda E: E.activation(
        out=p.rh[:], in_=csh_sl, func=AF.Exp, bias=totcol, scale=-1.0), "sA")
    dcv = p.run("scalar", lambda E: E.activation(
        out=p.dcol[:], in_=totcol, func=AF.Exp, bias=0.0, scale=1.0), "sA")
    p.wait("tensor", "sA", rhv)
    rtv = p.run("tensor", lambda E: E.transpose(
        p.tr_ps[:, 0:64], p.rh[:], p.eye128_b[0:64, 0:64]), "sT")
    p.wait("vector", "sT", rtv)
    rtev = p.run("vector", lambda E: E.tensor_copy(rt_sl, p.tr_ps[:, 0:64]), "sV")
    p.wait("tensor", "sA", dcv)
    p.wait("tensor", "sV", cstv)
    dtv = p.run("tensor", lambda E: E.transpose(
        p.ms_ps[0:64, 64:128], p.dcol[:].broadcast_to([NH, 64]), p.eye64_f[:]), "sT")
    p.wait("vector", "sT", dtv)
    dtev = p.run("vector", lambda E: E.tensor_copy(p.dtile[:], p.ms_ps[0:64, 64:128]), "sV")
    p.wait("tensor", "sV", rtev)
    kctv = p.run("tensor", lambda E: E.transpose(
        p.tr_ps[:, 64:128], kcT, p.eye128_b[0:64, 0:64]), "sT")
    p.wait("vector", "sT", kctv)
    kcev = p.run("vector", lambda E: E.tensor_copy(p.kc_sb[:, 0:64], p.tr_ps[:, 64:128]), "sV")
    p.wait("tensor", "sV", dtev)
    scv = p.run("tensor", lambda E: E.matmul(
        p.ms_ps[:, 128:256], kcT, qcT, start=True, stop=True), "sT")
    p.wait("vector", "sT", scv)
    smv = p.run("vector", lambda E: E.tensor_tensor(
        out=smt_sl, in0=p.ms_ps[:, 128:256], in1=p.mask01_b[:], op=ALU.mult), "sV")
    yq_a = p.psA[:, 0:1024]
    yq_b = p.psA[:, 1024:2048]
    st_q = p.psB[0:64, 0:1024]
    last_c2 = None
    last_u2 = None
    dslot_free = {}
    csb_free = None
    last_rv_users = None
    for q in range(4):
        dslot = p.datt[:]
        p.selfsync("vector")
        p.wait("vector", "sV", rtev)
        if last_rv_users is not None:
            p.wait("vector", "sT", last_rv_users)
        rvv = p.run("vector", lambda E, q=q: E.tensor_tensor(
            out=p.rv[:].rearrange("z (h e) -> z h e", h=16),
            in0=vtc[:, sl(q * 1024, 1024)].rearrange("z (h e) -> z h e", h=16),
            in1=rt_sl[:, sl(q * 16, 16)].rearrange("z (h o) -> z h o", o=1).broadcast_to([128, 16, HD]),
            op=ALU.mult), "sV")
        vdv = p.run("vector", lambda E, q=q: E.tensor_tensor(
            out=p.vd[:].rearrange("z (h e) -> z h e", h=16),
            in0=vtc[:, sl(q * 1024, 1024)].rearrange("z (h e) -> z h e", h=16),
            in1=p.drep[:, sl(q * 16, 16)].rearrange("z (h o) -> z h o", o=1).broadcast_to([128, 16, HD]),
            op=ALU.mult), "sV")
        if csb_free is not None:
            p.wait("gpsimd", "sV", csb_free)
        p.wait("gpsimd", "sFL", flv)
        cbv = p.dma("gpsimd", p.csb[:],
                    p.csflat[0].rearrange("h s -> (h s)")[sl(q * 2048, 2048)]
                    .rearrange("(a f) -> a f", a=1).broadcast_to([128, 2048]), sem="sCB")
        p.wait("vector", "sCB", cbv)
        dsv = p.run("vector", lambda E, q=q: E.tensor_tensor(
            out=p.darg[:].rearrange("z (h s) -> z h s", h=16),
            in0=p.csb[:].rearrange("z (h s) -> z h s", h=16),
            in1=cst_sl[:, sl(q * 16, 16)].rearrange("z (h o) -> z h o", o=1).broadcast_to([128, 16, S]),
            op=ALU.subtract), "sV")
        csb_free = dsv
        if dslot_free:
            p.wait("scalar", "sT", dslot_free[0])
        p.wait("scalar", "sV", dsv)
        absv = p.run("scalar", lambda E, dslot=dslot: E.activation(
            out=dslot, in_=p.darg[:], func=AF.Abs, bias=0.0, scale=1.0), "sA")
        p.wait("scalar", "sA", absv)
        expv = p.run("scalar", lambda E, dslot=dslot: E.activation(
            out=p.darg[:], in_=dslot, func=AF.Exp, bias=0.0, scale=-1.0), "sA")
        p.wait("vector", "sA", expv)
        p.wait("vector", "sV", smv)
        attv = p.run("vector", lambda E, dslot=dslot: E.tensor_tensor(
            out=dslot.rearrange("z (h s) -> z h s", h=16),
            in0=p.darg[:].rearrange("z (h s) -> z h s", h=16),
            in1=smt_sl.rearrange("z (o s) -> z o s", o=1).broadcast_to([128, 16, S]),
            op=ALU.mult), "sV")

        if last_c2 is not None:
            p.wait("tensor", "sV", last_c2)
        for j in range(2):
            p.run("tensor", lambda E, q=q, j=j: E.matmul(
                yq_b[:, sl(j * 512, 512)], qcT, p.h_bf[:, sl(q * 1024 + j * 512, 512)],
                start=True, stop=True), None)
        iv = p.run("tensor", lambda E: E.drain(), "sT")
        p.wait("tensor", "sV", attv)
        p.wait("tensor", "sV", vdv)
        for j in range(2):
            p.run("tensor", lambda E, q=q, j=j: E.matmul(
                yq_a[:, sl(j * 512, 512)], p.eye128_b[:], p.vd[:, sl(j * 512, 512)],
                start=True, stop=False), None)
            for i in range(8):
                hl = j * 8 + i
                p.run("tensor", lambda E, q=q, j=j, i=i, hl=hl, dslot=dslot: E.matmul(
                    yq_a[:, sl(j * 512 + i * 64, 64)], dslot[:, sl(hl * 128, 128)],
                    vtc[:, sl((q * 16 + hl) * 64, 64)],
                    start=False, stop=(i == 7)), None)
        av2 = p.run("tensor", lambda E: E.drain(), "sT")
        dslot_free[0] = av2
        p.wait("tensor", "sV", rvv)
        p.wait("tensor", "sV", kcev)
        if last_u2 is not None:
            p.wait("tensor", "sV", last_u2)
        for j in range(2):
            p.run("tensor", lambda E, q=q, j=j: E.matmul(
                st_q[:, sl(j * 512, 512)], p.kc_sb[:, 0:64], p.rv[:, sl(j * 512, 512)],
                start=True, stop=True), None)
        stv = p.run("tensor", lambda E: E.drain(), "sT")
        last_rv_users = stv

        p.wait("vector", "sT", av2)
        p.wait("vector", "sA", etv)
        c1 = p.run("vector", lambda E, q=q: E.tensor_tensor(
            out=p.y_tmp[:].rearrange("z (h e) -> z h e", h=16),
            in0=yq_b.rearrange("z (h e) -> z h e", h=16),
            in1=et_sl[:, sl(q * 16, 16)].rearrange("z (h o) -> z h o", o=1).broadcast_to([128, 16, HD]),
            op=ALU.mult), "sV")
        p.wait("vector", "sV", c1)
        c2 = p.run("vector", lambda E, q=q: E.tensor_tensor(
            out=p.y_sb[:, sl(q * 1024, 1024)], in0=p.y_tmp[:], in1=yq_a, op=ALU.add), "sV")
        last_c2 = c2
        p.wait("vector", "sT", stv)
        p.wait("vector", "sV", dtev)
        u1 = p.run("vector", lambda E, q=q: E.tensor_tensor(
            out=p.h_sb[:, sl(q * 1024, 1024)].rearrange("n (h e) -> n h e", h=16),
            in0=p.h_sb[:, sl(q * 1024, 1024)].rearrange("n (h e) -> n h e", h=16),
            in1=p.dtile[:, sl(q * 16, 16)].rearrange("n (h o) -> n h o", o=1).broadcast_to([NST, 16, HD]),
            op=ALU.mult), "sV")
        p.wait("vector", "sV", u1)
        u2 = p.run("vector", lambda E, q=q: E.tensor_tensor(
            out=p.h_sb[:, sl(q * 1024, 1024)], in0=p.h_sb[:, sl(q * 1024, 1024)],
            in1=st_q, op=ALU.add), "sV")
        last_u2 = u2
        p.wait("scalar", "sV", u2)
        p.wait("scalar", "sT", iv)
        p.run("scalar", lambda E, q=q: E.activation(
            out=p.h_bf[:, sl(q * 1024, 1024)], in_=p.h_sb[:, sl(q * 1024, 1024)],
            func=AF.Copy, bias=0.0, scale=1.0), "sA")

    p.wait("scalar", "sV", last_c2)
    sqv = p.run("scalar", lambda E: E.activation(
        out=p.ygt[:], in_=p.y_sb[:], func=AF.Square, bias=0.0, scale=1.0,
        accum_out=p.ss_col[:]), "sA")
    p.wait("vector", "sA", sqv)
    r1 = p.run("vector", lambda E: E.tensor_scalar(
        out=p.ms_col[:], in0=p.ss_col[:], scalar1=1.0 / DI, scalar2=1e-6,
        op0=ALU.mult, op1=ALU.add), "sV")
    p.wait("vector", "sV", r1)
    r2 = p.run("vector", lambda E: E.reciprocal(p.ss_col[:], p.ms_col[:]), "sV")
    p.wait("scalar", "sV", r2)
    rsv = p.run("scalar", lambda E: E.activation(
        out=p.rstd2[:], in_=p.ss_col[:], func=AF.Sqrt, bias=0.0, scale=1.0), "sA")
    p.wait("vector", "sA", rsv)
    p.wait("vector", "sV", last_c2)
    p.wait("vector", "sZB", zv)
    gv = p.run("vector", lambda E: E.scalar_tensor_tensor(
        out=p.ygt[:], in0=p.y_sb[:], scalar=p.rstd2[:], in1=p.zbuf[:, 0:DI],
        op0=ALU.mult, op1=ALU.mult), "sV")
    p.wait("tensor", "sV", gv)
    lastev = None
    for fg in range(8):
        if lastev is not None:
            p.wait("tensor", "sA", lastev)
        for f4 in range(4):
            fc = fg * 4 + f4
            p.run("tensor", lambda E, fc=fc, f4=f4: E.transpose(
                p.tr_ps[:, sl(f4 * 128, 128)], p.ygt[:, sl(fc * 128, 128)], p.eye128_b[:]), None)
        tv = p.run("tensor", lambda E: E.drain(), "sT")
        p.wait("scalar", "sT", tv)
        dst = (p.slotA if fg < 4 else p.slotB)
        lastev = p.run("scalar", lambda E, dst=dst, fg=fg, c=c: E.activation(
            out=dst.rearrange("z (f x) -> z f x", f=16)[:, sl((fg % 4) * 4, 4), sl(c * 128, 128)],
            in_=p.tr_ps[:].rearrange("z (f x) -> z f x", f=4),
            func=AF.Copy, bias=0.0, scale=1.0), "sA")


def emit_outproj(kb: "KB", t):
    p = kb
    uld = p.dma("gpsimd", p.xslot[:].rearrange("z (a d_) -> z a d_", a=NCH),
                p.u[sl(t * TT, TT), :].rearrange("(a z) d_ -> z a d_", z=128), sem="sU")
    ofree = {}
    evs = {}

    def stream(cb, kg):
        par = (cb * 16 + kg) % 2
        if par in ofree:
            p.wait("sync", "sT", ofree[par])
        sname = "owA" if par == 0 else "owB"
        p.dma("sync", p.owbuf[:, sl(par * 1024, 1024)].rearrange("z (k c) -> z k c", k=2),
              p.wsc_out[sl(kg * 2, 2), :, sl(cb * 512, 512)].rearrange("k q c -> q k c"), sem=sname)
        return (sname, p.v(sname))

    sems = {(0, 0): stream(0, 0), (0, 1): stream(0, 1)}
    for cb in range(4):
        for kg in range(16):
            par = (cb * 16 + kg) % 2
            p.wait("tensor", *sems[(cb, kg)])
            nxt = (cb, kg + 2) if kg + 2 < 16 else (cb + 1, kg + 2 - 16)
            for tsub in range(NCH):
                psv = p.psA[:, sl(tsub * 512, 512)]
                if kg == 0 and tsub in evs:
                    p.wait("tensor", "sV", evs[tsub])
                for k4 in range(2):
                    k = kg * 2 + k4
                    p.run("tensor", lambda E, k=k, k4=k4, kg=kg, par=par, tsub=tsub, psv=psv: E.matmul(
                        psv, p.ygf_slice(k, tsub), p.owbuf[:, sl(par * 1024 + k4 * 512, 512)],
                        start=(kg == 0 and k4 == 0), stop=(kg == 15 and k4 == 1)), None)
            mv = p.run("tensor", lambda E: E.drain(), "sT")
            ofree[par] = mv
            if nxt[0] < 4:
                sems[nxt] = stream(*nxt)
        p.wait("vector", "sT", p.v("sT"))
        p.wait("vector", "sU", uld)
        for tsub in range(NCH):
            dst = p.xslot[:, sl(tsub * D + cb * 512, 512)]
            evs[tsub] = p.run("vector", lambda E, dst=dst, tsub=tsub: E.tensor_tensor(
                out=dst, in0=p.psA[:, sl(tsub * 512, 512)], in1=dst, op=ALU.add), "sV")


def emit_ffn(kb: "KB", t):
    p = kb
    n2f = p.slotA
    for tsub in range(NCH):
        p.selfsync("vector")
        p.selfsync("scalar")
        psv = p.psB[:, 0:128]
        for k in range(NK):
            p.run("tensor", lambda E, k=k, tsub=tsub, psv=psv: E.matmul(
                psv[:, 0:118], p.n2f_slice(k, tsub), p.wsmall_sb[:, sl(k * 118, 118)],
                start=(k == 0), stop=False), None)
        p.run("tensor", lambda E, psv=psv: E.matmul(
            psv[:, 0:118], p.ones_row[:], p.brow_sb[:], start=False, stop=True), None)
        smv = p.run("tensor", lambda E: E.drain(), "sT")
        p.wait("vector", "sT", smv)
        lg = psv[:, 16:22]
        v1 = p.run("vector", lambda E, lg=lg: E.reduce_max(
            out=p.m1c[:], in_=lg, axis=AX.X), "sV")
        p.wait("vector", "sV", v1)
        v2 = p.run("vector", lambda E: E.tensor_scalar(
            out=p.ms_col[:], in0=p.m1c[:], scalar1=-1.0, scalar2=None, op0=ALU.mult), "sV")
        p.wait("scalar", "sV", v2)
        a1 = p.run("scalar", lambda E, lg=lg: E.activation(
            out=p.eg[:], in_=lg, func=AF.Exp, bias=p.ms_col[:], scale=1.0,
            accum_out=p.sumg[:]), "sA")
        p.wait("vector", "sA", a1)
        v3 = p.run("vector", lambda E: E.reciprocal(p.rsg[:], p.sumg[:]), "sV")
        p.wait("vector", "sV", v3)
        v4 = p.run("vector", lambda E: E.tensor_scalar(
            out=p.gates[:], in0=p.eg[:], scalar1=p.rsg[:], scalar2=None, op0=ALU.mult), "sV")
        p.wait("vector", "sV", v4)
        v5 = p.run("vector", lambda E: E.reduce_max(
            out=p.m1c[:], in_=p.gates[:], axis=AX.X), "sV")
        p.wait("vector", "sV", v5)
        v6 = p.run("vector", lambda E: E.tensor_scalar(
            out=p.gm[:], in0=p.gates[:], scalar1=p.m1c[:], scalar2=-1e30,
            op0=ALU.is_ge, op1=ALU.mult), "sV")
        p.wait("vector", "sV", v6)
        v7 = p.run("vector", lambda E: E.tensor_tensor(
            out=p.gm[:], in0=p.gates[:], in1=p.gm[:], op=ALU.add), "sV")
        p.wait("vector", "sV", v7)
        v8 = p.run("vector", lambda E: E.reduce_max(
            out=p.m2c[:], in_=p.gm[:], axis=AX.X), "sV")
        p.wait("vector", "sV", v8)
        v9 = p.run("vector", lambda E: E.tensor_scalar(
            out=p.wpe[:], in0=p.gates[:], scalar1=p.m2c[:], scalar2=None, op0=ALU.is_ge), "sV")
        p.wait("vector", "sV", v9)
        v10 = p.run("vector", lambda E: E.tensor_tensor(
            out=p.wpe[:], in0=p.gates[:], in1=p.wpe[:], op=ALU.mult), "sV")
        p.wait("scalar", "sT", smv)
        a2 = p.run("scalar", lambda E, psv=psv: E.activation(
            out=p.ffn_small[:], in_=psv[:, 0:118], func=AF.Copy, bias=0.0, scale=1.0), "sA")
        p.wait("vector", "sA", a2)
        p.wait("vector", "sV", v10)
        s1 = p.run("vector", lambda E: E.tensor_tensor(
            out=p.stackt[:, 16:112].rearrange("z (e r) -> z e r", e=6),
            in0=p.ffn_small[:, 22:118].rearrange("z (e r) -> z e r", e=6),
            in1=p.wpe[:].rearrange("z (e o) -> z e o", o=1).broadcast_to([128, 6, 16]),
            op=ALU.mult), "sV")
        s2 = p.run("vector", lambda E: E.tensor_copy(
            p.stackt[:, 0:16], p.ffn_small[:, 0:16]), "sV")
        p.wait("tensor", "sV", s2)
        tvv = p.run("tensor", lambda E: E.transpose(
            p.tr_ps[0:113, 0:128], p.stackt[:], p.eye128_b[:]), "sT")
        p.wait("vector", "sT", tvv)
        sev = p.run("vector", lambda E, tsub=tsub: E.tensor_copy(
            p.stackT[:, sl(tsub * 128, 128)], p.tr_ps[0:113, 0:128]), "sV")
        p.wait("tensor", "sV", sev)

    bfree = {}
    evs = {}
    stv = {}

    def stream(cb, kg):
        par = (cb * 8 + kg) % 2
        if par in bfree:
            p.wait("sync", "sT", bfree[par])
        sname = "bwA" if par == 0 else "bwB"
        p.dma("sync", p.bwbuf[:, sl(par * 1024, 1024)].rearrange("z (k c) -> z k c", k=2),
              p.wsc_base[sl(kg * 2, 2), :, sl(cb * 512, 512)].rearrange("k q c -> q k c"), sem=sname)
        return (sname, p.v(sname))

    sems = {(0, 0): stream(0, 0), (0, 1): stream(0, 1)}
    for cb in range(4):
        for kg in range(8):
            par = (cb * 8 + kg) % 2
            p.wait("tensor", *sems[(cb, kg)])
            nxt = (cb, kg + 2) if kg + 2 < 8 else (cb + 1, kg + 2 - 8)
            for tsub in range(NCH):
                psv = p.psA[:, sl(tsub * 512, 512)]
                if kg == 0 and tsub in evs:
                    p.wait("tensor", "sV", evs[tsub])
                for k4 in range(2):
                    k = kg * 2 + k4
                    p.run("tensor", lambda E, k=k, k4=k4, kg=kg, par=par, tsub=tsub, psv=psv: E.matmul(
                        psv, p.n2f_slice(k, tsub), p.bwbuf[:, sl(par * 1024 + k4 * 512, 512)],
                        start=(kg == 0 and k4 == 0), stop=False), None)
            mv = p.run("tensor", lambda E: E.drain(), "sT")
            bfree[par] = mv
            if nxt[0] < 4:
                sems[nxt] = stream(*nxt)
        for tsub in range(NCH):
            psv = p.psA[:, sl(tsub * 512, 512)]
            p.run("tensor", lambda E, tsub=tsub, psv=psv, cb=cb: E.matmul(
                psv, p.stackT[:, sl(tsub * 128, 128)], p.bdown_sb[:, sl(cb * 512, 512)],
                start=False, stop=True), None)
        dv = p.run("tensor", lambda E: E.drain(), "sT")
        p.wait("vector", "sT", dv)
        if cb > 0:
            p.wait("vector", "sSO", stv[cb - 1])
        for tsub in range(NCH):
            evs[tsub] = p.run("vector", lambda E, tsub=tsub, cb=cb: E.tensor_tensor(
                out=p.x2_sb[:, sl(tsub * 512, 512)], in0=p.psA[:, sl(tsub * 512, 512)],
                in1=p.xslot[:, sl(tsub * D + cb * 512, 512)], op=ALU.add), "sV")
        p.wait("gpsimd", "sV", max(evs.values()))
        for tsub in range(NCH):
            p.dma("gpsimd",
                  p.out[sl(t * TT + tsub * 128, 128), sl(cb * 512, 512)],
                  p.x2_sb[:, sl(tsub * 512, 512)], sem="sSO")
        stv[cb] = p.v("sSO")


def build_full(ntiles=NTILE, debug=()):
    kb = KB(debug=debug)
    p = kb
    emit_prep(kb)
    for t in range(ntiles):
        uld = p.dma("gpsimd", p.xslot[:].rearrange("z (a d_) -> z a d_", a=NCH),
                    p.u[sl(t * TT, TT), :].rearrange("(a z) d_ -> z a d_", z=128), sem="sU")
        p.wait("vector", "sU", uld)
        p.wait("scalar", "sU", uld)
        emit_ln(kb, t, kb.slotA, "ln1")
        p.barrier()
        kb.tap("normed_t%d" % t, kb.slotA[:])
        emit_inproj_f(kb, t)
        p.barrier()
        kb.tap("vt_t%d" % t, kb.vt[:])
        kb.tap("bm_t%d" % t, kb.bm_b[:])
        kb.tap("cm_t%d" % t, kb.cm_b[:])
        kb.tap("la_t%d" % t, kb.la[:])
        emit_inproj_z(kb, t)
        p.barrier()
        for c in range(NCH):
            emit_ssd_chunk(kb, t, c)
            p.barrier()
            kb.tap("ysb_t%d_c%d" % (t, c), kb.y_sb[:])
            kb.tap("ygt_t%d_c%d" % (t, c), kb.ygt[:])
        emit_outproj(kb, t)
        p.barrier()
        kb.tap("x1_t%d" % t, kb.xslot[:])
        emit_ln(kb, t, kb.slotA, "ln2")
        p.barrier()
        emit_ffn(kb, t)
        p.barrier()
    return kb


def _make_in_map(inputs, core):
    f = np.ascontiguousarray
    u = f(np.asarray(inputs["L0_attn_u"][core], np.float32))
    w_small = np.concatenate([
        np.asarray(inputs["L0_ffn_A_s"], np.float32),
        np.asarray(inputs["L0_ffn_W_router"], np.float32),
        np.asarray(inputs["L0_ffn_A_r"], np.float32).transpose(1, 0, 2).reshape(2048, 96),
    ], axis=1)
    b_down = np.concatenate([
        np.asarray(inputs["L0_ffn_B_s"], np.float32),
        np.asarray(inputs["L0_ffn_B_r"], np.float32).reshape(96, 2048),
    ], axis=0)
    brow = np.zeros((1, 118), np.float32)
    brow[0, 16:22] = np.asarray(inputs["L0_ffn_b_router"], np.float32)
    mask01 = np.triu(np.ones((128, 128), np.float32))
    return dict(
        u=u,
        w_in=f(np.asarray(inputs["L0_attn_in_proj_w"], np.float32)),
        convw=f(np.asarray(inputs["L0_attn_conv_weight"], np.float32)
                .reshape(4, 32, 128).transpose(2, 1, 0).reshape(128, 128)),
        convb=f(np.asarray(inputs["L0_attn_conv_bias"], np.float32).reshape(32, 128).T),
        A_log=np.asarray(inputs["L0_attn_A_log"], np.float32).reshape(-1, 1),
        dt_bias=np.asarray(inputs["L0_attn_dt_bias"], np.float32).reshape(-1, 1),
        D_param=np.asarray(inputs["L0_attn_D_param"], np.float32).reshape(1, -1),
        norm_scale=f(np.asarray(inputs["L0_attn_norm_scale"], np.float32).reshape(32, 128).T),
        w_out=f(np.asarray(inputs["L0_attn_out_proj_w"], np.float32)),
        w_base=f(np.asarray(inputs["L0_ffn_W_base"], np.float32)),
        b_base=np.asarray(inputs["L0_ffn_b_base"], np.float32).reshape(1, -1),
        w_small=f(w_small),
        b_down=f(b_down),
        brow=brow,
        eye128=np.eye(128, dtype=np.float32),
        mask01=mask01,
    )


def kernel(**inputs):
    from concourse.bass_utils import run_bass_kernel_spmd
    kb = build_full(ntiles=NTILE)
    nc = kb.finalize()
    in_maps = [_make_in_map(inputs, c) for c in range(8)]
    res = run_bass_kernel_spmd(nc, in_maps, core_ids=list(range(8)))
    out = np.stack([np.asarray(r["out"], np.float32) for r in res.results], axis=0)
    return out


# version 02
# speedup vs baseline: 1.0691x; 1.0691x over previous
import numpy as np
import concourse.bass as bass
import concourse.mybir as mybir

F32 = mybir.dt.float32
BF16 = mybir.dt.bfloat16
AF = mybir.ActivationFunctionType
ALU = mybir.AluOpType
AX = mybir.AxisListType

L, D, DI, NH, HD, NST = 4096, 2048, 4096, 64, 64, 64
TT, S = 512, 128
NTILE, NCH = L // TT, TT // S
DIP = 8384
NK = D // 128
NKI = DI // 128
ENGS = ("sync", "vector", "scalar", "tensor", "gpsimd")


def sl(off, n):
    return slice(off, off + n)


class Sem:
    __slots__ = ("h", "val")

    def __init__(self, nc):
        self.h = nc.alloc_semaphore()
        self.val = 0


class KB:
    def __init__(self, debug=()):
        nc = self.nc = bass.Bass("TRN2", target_bir_lowering=False)
        self.streams = {e: [] for e in ENGS}
        self.debug = dict(debug)
        self.dbg_t = {}
        self.dbg_done = set()

        din = lambda n, sh: nc.dram_tensor(n, sh, F32, kind="ExternalInput").ap()
        self.u = din("u", [L, D])
        self.w_in = din("w_in", [D, DIP])
        self.convw = din("convw", [128, 128])
        self.convb = din("convb", [128, 32])
        self.A_log = din("A_log", [NH, 1])
        self.dt_bias = din("dt_bias", [NH, 1])
        self.D_param = din("D_param", [1, NH])
        self.norm_scale = din("norm_scale", [128, 32])
        self.w_out = din("w_out", [DI, D])
        self.w_base = din("w_base", [D, D])
        self.b_base = din("b_base", [1, D])
        self.w_small = din("w_small", [D, 118])
        self.b_down = din("b_down", [112, D])
        self.brow = din("brow", [1, 118])
        self.eye128 = din("eye128", [128, 128])
        self.mask01 = din("mask01", [128, 128])
        self.out = nc.dram_tensor("out", [L, D], F32, kind="ExternalOutput").ap()
        for name, (shape, dt_) in self.debug.items():
            self.dbg_t[name] = nc.dram_tensor("dbg_" + name, shape, dt_, kind="ExternalOutput").ap()

        dint = lambda n, sh, dt_: nc.dram_tensor(n, sh, dt_, kind="Internal").ap()
        self.wsc_in = dint("wsc_in", [NK, 128, DIP], BF16)
        self.wsc_out = dint("wsc_out", [NKI, 128, D], BF16)
        self.wsc_base = dint("wsc_base", [NK, 128, D], BF16)
        self.wsc_diag = dint("wsc_diag", [32, 4, 128, 128], BF16)
        self.zsc = dint("zsc", [NCH, 128, DI], BF16)
        self.csflat = dint("csflat", [2, NH, S], F32)
        self.rowsc = dint("rowsc", [1, D], BF16)

        DTSZ = {F32: 4, BF16: 2}
        ARENA = 206848
        self.arena = nc.alloc_sbuf_tensor("arena", [128, ARENA], mybir.dt.uint8)
        self._abase = nc.lookup_mloc(self.arena).addr
        self._cursor = 0
        self._mstart = {}
        self._table = []

        def A(n, sh, dt_, overlay=None):
            nbytes = sh[-1] * DTSZ[dt_]
            for d in sh[1:-1]:
                nbytes *= d
            nbytes = (nbytes + 31) // 32 * 32
            if overlay is not None:
                start = overlay
            else:
                start = self._cursor
                self._cursor += nbytes
            assert start + nbytes <= ARENA, f"SBUF arena overflow at {n}: {start + nbytes}"
            self._table.append((n, start, nbytes, overlay is not None))
            h = nc.virtual_sbuf_tensor(n, sh, dt_)
            nc._tensor(h.name, h.shape, h.dtype, type="SB")
            mloc = nc.lookup_mloc(h)
            mloc.allocated = True
            mloc.addr = self._abase + start
            ap = h.ap()
            self._mstart[n] = start
            return ap

        self.eye128_b = A("eye128_b", [128, 128], BF16)
        self.eye64_f = A("eye64_f", [64, 64], F32)
        self.mask01_b = A("mask01_b", [128, 128], BF16)
        self.ns_cols = A("ns_cols", [128, NKI], F32)
        self.cb_cols = A("cb_cols", [128, 32], F32)
        self.negdtb = A("negdtb", [NH, 1], F32)
        self.expA = A("expA", [NH, 1], F32)
        self.drep = A("drep", [128, NH], BF16)
        self.wsmall_sb = A("wsmall_sb", [128, NK * 118], BF16)
        self.bdown_sb = A("bdown_sb", [113, D], BF16)
        self.brow_sb = A("brow_sb", [1, 118], BF16)
        self.ones_row = A("ones_row", [1, 128], BF16)
        self.cwT = A("cwT", [128, 128], F32)

        self.vt = A("vt", [128, NCH * DI], BF16)
        self.xslot = A("xslot", [128, NCH * D], F32, overlay=self._mstart["vt"])
        self.slotA = A("slotA", [128, 8192], BF16)
        self.slotB = A("slotB", [128, 8192], BF16)
        self.zbuf = None
        self.xrr = A("xrr", [128, 6 * 515], BF16)
        self.xcr = A("xcr", [128, 4 * 512], BF16)
        self.zbuf = A("zbuf", [128, DI], BF16, overlay=self._mstart["xrr"])
        self.darg = A("darg", [128, 2048], BF16)
        self.datt = A("datt", [128, 2048], BF16)
        self.csb = A("csb", [128, 2048], F32)
        self.ygt = A("ygt", [128, DI], BF16)
        self.prep_b = self.ygt[:, 0:2048]
        self.ntok = self.ygt[:, 2048:4096]
        self.rv = A("rv", [128, 1024], BF16)
        self.vd = A("vd", [128, 1024], BF16)
        self.y_tmp = A("y_tmp", [128, 1024], F32)
        self.scrF = A("scrF", [128, DI], F32)
        self.y_sb = self.scrF[:, 0:DI]
        self.lns = self.scrF[:, 0:D]
        self.x2_sb = self.scrF[:, D:2 * D]
        self.prep_f = self.scrF[:, 0:DI]
        self.h_sb = A("h_sb", [NST, DI], F32)
        self.h_bf = A("h_bf", [NST, DI], BF16)

        self.wbuf = A("wbuf", [128, 2 * NK * 256], BF16)
        self.zwbuf = A("zwbuf", [128, 2 * 2 * 512], BF16)
        self.owbuf = A("owbuf", [128, 2 * 2 * 512], BF16)
        self.bwbuf = A("bwbuf", [128, 2 * 2 * 512], BF16)
        self.dgbuf = A("dgbuf", [128, 2 * 4 * 128], BF16)

        self.smt = A("smt", [128, S], BF16)
        self.bm_b = A("bm_b", [64, TT], BF16)
        self.cm_b = A("cm_b", [64, TT], BF16)
        self.rh = A("rh", [NH, S], BF16)
        self.rt = A("rt", [128, NH], BF16)
        self.kc_sb = A("kc_sb", [128, NST], BF16)
        self.hist = A("hist", [128, 32 * 3], BF16)
        self.stackt = A("stackt", [128, 113], BF16)
        self.stackT = A("stackT", [113, TT], BF16)
        self.ffn_small = A("ffn_small", [128, 118], BF16)
        self.csh = A("csh", [NH, S], F32)
        self.cst = A("cst", [128, NH], F32)
        self.et = A("et", [128, NH], F32)
        self.la = A("la", [NH, TT], F32)
        self.dtile = A("dtile", [NST, NH], F32)
        self.dcol = A("dcol", [NH, 1], F32)
        self.gates = A("gates", [128, 6], F32)
        self.eg = A("eg", [128, 6], F32)
        self.gm = A("gm", [128, 6], F32)
        self.wpe = A("wpe", [128, 6], F32)
        cols = lambda n: A(n, [128, 1], F32)
        self.sum_col, self.ssq_col, self.ms_col = cols("sum_col"), cols("ssq_col"), cols("ms_col")
        self.rstd_col, self.nmr_col = cols("rstd_col"), cols("nmr_col")
        self.ss_col, self.rstd2 = cols("ss_col"), cols("rstd2")
        self.m1c, self.m2c, self.sumg, self.rsg = cols("m1c"), cols("m2c"), cols("sumg"), cols("rsg")

        Pp = lambda n, sh, dt_: nc.alloc_psum_tensor(n, sh, dt_).ap()
        self.psA = Pp("psA", [128, 2048], F32)
        self.psB = Pp("psB", [128, 1024], F32)
        self.ms_ps = Pp("ms_ps", [128, 512], F32)
        self.tr_ps = Pp("tr_ps", [128, 512], BF16)

        self.S_ = {n: Sem(nc) for n in ("sW", "sG", "sV", "sA", "sT")}
        self.DMA_SEMS = {"sW", "sG"}
        self.gfree = {}
        self._zbuf_free = 0
        self._csb_free = 0
        self._csflat_free = 0
        self._zsc_done = 0
        self._prep2_v = 0
        self._prep2_o = 0
        self._conv_evac = {}

    def normed(self):
        return self.slotA

    def n2f_slice(self, k, tsub):
        return self.slotA[:, sl(k * 512 + tsub * 128, 128)]

    def ygf_slice(self, k, tsub):
        base = k * 512 + tsub * 128
        return (self.slotA if k < 16 else self.slotB)[:, sl(base - (8192 if k >= 16 else 0), 128)]

    def emit(self, eng, fn):
        self.streams[eng].append(fn)

    def sem(self, name):
        if name not in self.S_:
            self.S_[name] = Sem(self.nc)
        return self.S_[name]

    def wait(self, eng, semname, val):
        s = self.sem(semname)
        self.emit(eng, lambda E, h=s.h, v=val: E.wait_ge(h, v))

    def run(self, eng, fn, inc=None, n=1):
        if inc is not None:
            s = self.sem(inc)
            s.val += n
            self.emit(eng, lambda E, fn=fn, h=s.h, n=n: fn(E).then_inc(h, n))
            return s.val
        self.emit(eng, lambda E, fn=fn: fn(E))
        return None

    def dma(self, eng, out, in_, slow=False, sem=None):
        inc = sem or {"sync": "sW", "gpsimd": "sG"}[eng]
        self.DMA_SEMS.add(inc)
        return self.run(eng, lambda E, o=out, i=in_, s=slow: E.dma_start(
            out=o, in_=i, allow_slow_non_contiguous=s), inc, 16)

    def v(self, name):
        return self.sem(name).val

    def selfsync(self, eng):
        em = {"vector": "sV", "scalar": "sA", "tensor": "sT"}[eng]
        self.wait(eng, em, self.S_[em].val)

    def barrier(self, engines=("vector", "scalar", "tensor"), dma=(), full=False):
        em = {"vector": "sV", "scalar": "sA", "tensor": "sT"}
        for e in engines:
            self.run(e, lambda E: E.drain(), em[e], 1)
        dma = tuple(self.DMA_SEMS) if full else tuple(dma)
        vals = {n: self.v(n) for n in list(em.values()) + list(dma)}
        targets = [em[e] for e in engines] + list(dma)
        waiters = engines + (("sync", "gpsimd") if full else ())
        for e in waiters:
            for sn in targets:
                self.wait(e, sn, vals[sn])

    def tap(self, name, src_ap):
        if name in self.dbg_t and name not in self.dbg_done:
            self.dbg_done.add(name)
            self.barrier(full=True)
            self.dma("gpsimd", self.dbg_t[name], src_ap, sem="sSO")
            self.barrier(full=True)

    def finalize(self):
        for sn in sorted(self.DMA_SEMS):
            self.wait("gpsimd", sn, self.v(sn))
        nc = self.nc
        with nc.Block() as block:
            for eng in ENGS:
                stream = self.streams[eng]
                dec = getattr(block, eng)

                def make(stream):
                    def body(E):
                        for fn in stream:
                            fn(E)
                    return body
                dec(make(stream))
        return nc



def emit_prep(kb: "KB"):
    p = kb
    st = p.scrF
    p.dma("gpsimd", st[:, 0:128], p.eye128)
    p.dma("gpsimd", p.eye64_f, p.eye128[0:64, 0:64])
    p.dma("gpsimd", st[:, 128:256], p.mask01)
    p.dma("gpsimd", p.ns_cols, p.norm_scale)
    p.dma("gpsimd", p.cb_cols, p.convb)
    p.dma("gpsimd", st[0:64, 2208:2209], p.dt_bias)
    p.dma("gpsimd", st[0:64, 2209:2210], p.A_log)
    p.dma("gpsimd", st[:, 256:320],
          p.D_param.rearrange("a (o h) -> a o h", o=1).broadcast_to([1, 128, NH]))
    p.dma("gpsimd", st[:, 320:320 + 16 * 118].rearrange("q (k c) -> q k c", k=16),
          p.w_small.rearrange("(k q) c -> q k c", q=128))
    p.dma("gpsimd", p.cwT, p.convw)
    gv = p.v("sG")
    p.wait("vector", "sG", gv)
    p.run("vector", lambda E: E.tensor_copy(p.eye128_b[:], st[:, 0:128]), "sV")
    p.run("vector", lambda E: E.tensor_copy(p.mask01_b[:], st[:, 128:256]), "sV")
    p.run("vector", lambda E: E.tensor_copy(p.drep[:], st[:, 256:320]), "sV")
    p.run("vector", lambda E: E.tensor_copy(p.wsmall_sb[:], st[:, 320:320 + 16 * 118]), "sV")
    p.run("vector", lambda E: E.tensor_scalar(out=p.negdtb[:], in0=st[0:64, 2208:2209],
                                              scalar1=-1.0, scalar2=None, op0=ALU.mult), "sV")
    p.wait("scalar", "sG", gv)
    expav = p.run("scalar", lambda E: E.activation(out=p.expA[:], in_=st[0:64, 2209:2210],
                                           func=AF.Exp, bias=0.0, scale=1.0), "sA")
    p.run("vector", lambda E: E.memset(p.ones_row[:], 1.0), "sV")
    p.run("vector", lambda E: E.memset(p.stackt[:, 112:113], 1.0), "sV")
    p.run("vector", lambda E: E.memset(p.h_sb[:], 0.0), "sV")
    p.run("vector", lambda E: E.memset(p.h_bf[:], 0.0), "sV")
    vfence = p.run("vector", lambda E: E.memset(p.hist[:], 0.0), "sV")

    p.wait("gpsimd", "sV", vfence)
    p.dma("gpsimd", st[0:112, 0:D], p.b_down)
    p.dma("gpsimd", p.csb[0:1, 0:D], p.b_base)
    p.dma("gpsimd", p.y_tmp[0:1, 0:118], p.brow)
    p.wait("vector", "sG", p.v("sG"))
    p.run("vector", lambda E: E.tensor_copy(p.bdown_sb[0:112, :], st[0:112, 0:D]), "sV")
    bb = p.run("vector", lambda E: E.tensor_copy(p.zbuf[0:1, 0:D], p.csb[0:1, 0:D]), "sV")
    vfence2 = p.run("vector", lambda E: E.tensor_copy(p.brow_sb[:], p.y_tmp[0:1, 0:118]), "sV")
    p.wait("gpsimd", "sV", bb)
    rw = p.dma("gpsimd", p.rowsc, p.zbuf[0:1, 0:D])
    p.wait("gpsimd", "sG", rw)
    p.dma("gpsimd", p.bdown_sb[112:113, :], p.rowsc)
    p.wait("sync", "sV", vfence2)
    p.wait("gpsimd", "sV", vfence2)

    p.wait("vector", "sV", p.v("sV"))
    for fc in range(32):
        p.wait("vector", "sW", p.v("sW"))
        for j in range(4):
            p.run("vector", lambda E, fc=fc, j=j: E.tensor_tensor(
                out=p.prep_b[:, sl(j * 128, 128)], in0=p.eye128_b[:],
                in1=p.cwT[:, fc * 4 + j:fc * 4 + j + 1].broadcast_to([128, 128]),
                op=ALU.mult), "sV")
        vv = p.v("sV")
        p.wait("sync", "sV", vv)
        p.dma("sync", p.wsc_diag[fc].rearrange("j q c -> q j c"), p.prep_b[:, 0:512].rearrange("q (j c) -> q j c", j=4))

    p.wait("vector", "sW", p.v("sW"))
    p.wait("sync", "sA", p.v("sA"))
    p.wait("sync", "sV", p.v("sV"))
    stF = [p.scrF[:, 0:2048], p.scrF[:, 2048:4096]]
    stB = [p.ygt[:, 0:2048], p.ygt[:, 2048:4096]]
    pI = ["pIA", "pIB"]
    pO = ["pOA", "pOB"]
    pieces = [(k, off, w) for k in range(NK)
              for (off, w) in [(0, 2048), (2048, 2048), (4096, 2048), (6144, 2048), (8192, 192)]]
    lastcast = {}
    for i, (k, off, w) in enumerate(pieces):
        pi = i % 2
        if i >= 2:
            p.wait("sync", "sV", lastcast[pi])
        iv = p.dma("sync", stF[pi][:, 0:w], p.w_in[sl(k * 128, 128), sl(off, w)], sem=pI[pi])
        p.wait("vector", pI[pi], iv)
        if i >= 2:
            p.wait("vector", pO[pi], p.v(pO[pi]))
        cv = p.run("vector", lambda E, pi=pi, w=w: E.tensor_copy(
            stB[pi][:, 0:w], stF[pi][:, 0:w]), "sV")
        lastcast[pi] = cv
        p.wait("gpsimd", "sV", cv)
        p.dma("gpsimd", p.wsc_in[k][:, sl(off, w)], stB[pi][:, 0:w], sem=pO[pi])
    p.wait("sync", "pOA", p.v("pOA"))
    p.wait("sync", "pOB", p.v("pOB"))

    p.barrier()
    kb.tap("wsc_probe", kb.wsc_in[0][:, 0:1024])


def emit_prep_outbase(kb: "KB"):
    p = kb
    stF = [p.csb[:, 0:1024], p.csb[:, 1024:2048]]
    stB = [p.datt[:, 0:1024], p.datt[:, 1024:2048]]
    pI = ["p2IA", "p2IB"]
    pO = ["p2OA", "p2OB"]
    jobs = []
    for k in range(NKI):
        for h in range(2):
            jobs.append(("out", k, h))
    for k in range(NK):
        for h in range(2):
            jobs.append(("base", k, h))
    lastcast = {}
    for i, (kind, k, h) in enumerate(jobs):
        pi = i % 2
        src = (p.w_out if kind == "out" else p.w_base)[sl(k * 128, 128), sl(h * 1024, 1024)]
        dst = (p.wsc_out[k] if kind == "out" else p.wsc_base[k])[:, sl(h * 1024, 1024)]
        if i >= 2:
            p.wait("sync", "sV", lastcast[pi])
        iv = p.dma("sync", stF[pi][:, 0:1024], src, sem=pI[pi])
        p.wait("vector", pI[pi], iv)
        if i >= 2:
            p.wait("vector", pO[pi], p.v(pO[pi]))
        if kind == "out":
            cv = p.run("vector", lambda E, pi=pi, k=k: E.tensor_scalar(
                out=stB[pi][:, 0:1024], in0=stF[pi][:, 0:1024],
                scalar1=p.ns_cols[:, k:k + 1], scalar2=None, op0=ALU.mult), "sV")
        else:
            cv = p.run("vector", lambda E, pi=pi: E.tensor_copy(
                stB[pi][:, 0:1024], stF[pi][:, 0:1024]), "sV")
        lastcast[pi] = cv
        p.wait("gpsimd", "sV", cv)
        p.dma("gpsimd", dst, stB[pi][:, 0:1024], sem=pO[pi])
    kb._prep2_v = max(lastcast.values())
    kb._prep2_o = p.v("p2OA")
    kb._prep2_o2 = p.v("p2OB")
    p.wait("sync", "p2OA", kb._prep2_o)
    p.wait("sync", "p2OB", kb._prep2_o2)


def emit_ln(kb: "KB", t, dst_slot, tap_prefix):
    p = kb
    for tsub in range(NCH):
        xs = p.xslot[:, sl(tsub * D, D)]
        p.selfsync("vector")
        p.selfsync("scalar")
        vs = p.run("vector", lambda E, xs=xs: E.tensor_reduce(
            out=p.sum_col[:], in_=xs, axis=AX.X, op=ALU.add), "sV")
        va = p.run("scalar", lambda E, xs=xs: E.activation(
            out=p.lns[:, 0:D], in_=xs, func=AF.Square, bias=0.0, scale=1.0,
            accum_out=p.ssq_col[:]), "sA")
        p.wait("vector", "sV", vs)
        vms = p.run("vector", lambda E: E.tensor_tensor(
            out=p.ms_col[:], in0=p.sum_col[:], in1=p.sum_col[:], op=ALU.mult), "sV")
        p.wait("vector", "sV", vms)
        p.wait("vector", "sA", va)
        vs2 = p.run("vector", lambda E: E.scalar_tensor_tensor(
            out=p.ms_col[:], in0=p.ms_col[:], scalar=-1.0 / D, in1=p.ssq_col[:],
            op0=ALU.mult, op1=ALU.add), "sV")
        p.wait("vector", "sV", vs2)
        vs2b = p.run("vector", lambda E: E.tensor_scalar(
            out=p.ms_col[:], in0=p.ms_col[:], scalar1=1.0 / D, scalar2=1e-5,
            op0=ALU.mult, op1=ALU.add), "sV")
        p.wait("vector", "sV", vs2b)
        vs2c = p.run("vector", lambda E: E.reciprocal(p.nmr_col[:], p.ms_col[:]), "sV")
        p.wait("scalar", "sV", vs2c)
        va2 = p.run("scalar", lambda E: E.activation(
            out=p.rstd_col[:], in_=p.nmr_col[:], func=AF.Sqrt, bias=0.0, scale=1.0), "sA")
        p.wait("vector", "sA", va2)
        vs3 = p.run("vector", lambda E: E.scalar_tensor_tensor(
            out=p.nmr_col[:], in0=p.sum_col[:], scalar=-1.0 / D, in1=p.rstd_col[:],
            op0=ALU.mult, op1=ALU.mult), "sV")
        p.wait("scalar", "sV", vs3)
        va3 = p.run("scalar", lambda E, xs=xs: E.activation(
            out=p.ntok[:], in_=xs,
            func=AF.Identity, bias=p.nmr_col[:], scale=p.rstd_col[:]), "sA")
        p.wait("tensor", "sA", va3)
        for kg in range(4):
            for k4 in range(4):
                k = kg * 4 + k4
                p.run("tensor", lambda E, k=k, k4=k4: E.transpose(
                    p.tr_ps[:, sl(k4 * 128, 128)], p.ntok[:, sl(k * 128, 128)],
                    p.eye128_b[:]), None)
            vt_ = p.run("tensor", lambda E: E.drain(), "sT")
            p.wait("scalar", "sT", vt_)
            va4 = p.run("scalar", lambda E, kg=kg, tsub=tsub: E.activation(
                out=dst_slot.rearrange("q (k c) -> q k c", k=16)[:, sl(kg * 4, 4), sl(tsub * 128, 128)],
                in_=p.tr_ps[:].rearrange("q (k c) -> q k c", k=4),
                func=AF.Copy, bias=0.0, scale=1.0), "sA")
            p.wait("tensor", "sA", va4)


def emit_inproj_f(kb: "KB", t):
    p = kb
    conv_done = {}
    vtr_done = {}
    evac_done = {}
    psum_evac = {}
    wbuf_free = p.gfree.setdefault("wbuf", {})
    dg_free = p.gfree.setdefault("dg", {})

    def stream_group(g):
        par = g % 2
        w = 256 if g < 16 else 192
        cols = sl(4096 + g * 256, w)
        if par in wbuf_free:
            p.wait("sync", "sT", wbuf_free[par])
        sname = "wPA" if par == 0 else "wPB"
        p.dma("sync", p.wbuf[:, sl(par * 4096, 16 * w)].rearrange("q (k c) -> q k c", k=16),
              p.wsc_in[:, :, cols].rearrange("k q c -> q k c"), sem=sname)
        return (sname, p.v(sname))

    def stream_diag(fc):
        par = fc % 2
        if par in dg_free:
            p.wait("sync", "sT", dg_free[par])
        sname = "dgA" if par == 0 else "dgB"
        p.dma("sync", p.dgbuf[:, sl(par * 512, 512)].rearrange("q (j c) -> q j c", j=4),
              p.wsc_diag[fc].rearrange("j q c -> q j c"), sem=sname)
        return (sname, p.v(sname))

    wsem = {0: stream_group(0), 1: stream_group(1)}
    dgsem = {0: stream_diag(0), 1: stream_diag(1)}

    for g in range(17):
        par = g % 2
        p.wait("tensor", *wsem[g])
        if g < 16:
            for fi in range(2):
                fc = 2 * g + fi
                psv = p.psA[:, sl(fi * 512, 512)]
                if fi in psum_evac:
                    p.wait("tensor", "sA", psum_evac[fi])
                for k in range(NK):
                    base = par * 4096 + k * 256 + fi * 128
                    p.run("tensor", lambda E, base=base, psv=psv, k=k: E.matmul(
                        psv, p.wbuf[:, sl(base, 128)], p.normed()[:, sl(k * 512, 512)],
                        start=(k == 0), stop=(k == NK - 1)), None)
                mmv = p.run("tensor", lambda E: E.drain(), "sT")
                if fi == 1:
                    wbuf_free[par] = mmv
                    if g + 2 <= 16:
                        wsem[g + 2] = stream_group(g + 2)
                if fc >= 6:
                    p.wait("scalar", "sT", conv_done[fc - 6])
                p.wait("scalar", "sT", mmv)
                ev = p.run("scalar", lambda E, fc=fc, psv=psv: E.activation(
                    out=p.xrr[:, sl((fc % 6) * 515 + 3, 512)], in_=psv,
                    func=AF.Copy, bias=0.0, scale=1.0), "sA")
                evac_done[fc] = ev
                psum_evac[fi] = ev
                if fc == 0:
                    p.wait("gpsimd", "sA", ev)
                    kb.tap("wbuf0", p.wbuf[:, 0:4096])
        else:
            p.wait("tensor", "sA", max(list(psum_evac.values()) + list(kb._conv_evac.values())))
            for j in range(3):
                psv = p.psA[0:64, sl(j * 512, 512)]
                coff = j * 64
                for k in range(NK):
                    base = par * 4096 + k * 192 + coff
                    p.run("tensor", lambda E, base=base, psv=psv, k=k: E.matmul(
                        psv, p.wbuf[:, sl(base, 64)], p.normed()[:, sl(k * 512, 512)],
                        start=(k == 0), stop=(k == NK - 1)), None)
            bcv = p.run("tensor", lambda E: E.drain(), "sT")
            wbuf_free[par] = bcv
            p.wait("vector", "sT", bcv)
            p.wait("scalar", "sT", bcv)
            for j, dst in [(0, p.bm_b), (1, p.cm_b)]:
                psv = p.psA[0:64, sl(j * 512, 512)]
                v1 = p.run("vector", lambda E, psv=psv: E.tensor_scalar(
                    out=p.lns[0:64, 0:512], in0=psv, scalar1=0.0, scalar2=None,
                    op0=ALU.min), "sV")
                p.wait("scalar", "sV", v1)
                a1 = p.run("scalar", lambda E: E.activation(
                    out=p.lns[0:64, 512:1024], in_=p.lns[0:64, 0:512],
                    func=AF.Exp, bias=0.0, scale=1.0), "sA")
                p.wait("vector", "sA", a1)
                p.run("vector", lambda E, psv=psv, dst=dst: E.scalar_tensor_tensor(
                    out=dst[:, 0:512], in0=psv, scalar=1.0, in1=p.lns[0:64, 512:1024],
                    op0=ALU.add, op1=ALU.max), "sV")
            psv = p.psA[0:64, sl(2 * 512, 512)]
            sgv = p.run("scalar", lambda E, psv=psv: E.activation(
                out=p.lns[0:64, 1024:1536], in_=psv,
                func=AF.Sigmoid, bias=p.negdtb[:], scale=-1.0), "sA")
            p.wait("scalar", "sA", sgv)
            a4 = p.run("scalar", lambda E: E.activation(
                out=p.lns[0:64, 1536:2048], in_=p.lns[0:64, 1024:1536],
                func=AF.Ln, bias=0.0, scale=1.0), "sA")
            p.wait("vector", "sA", a4)
            p.run("vector", lambda E: E.tensor_scalar(
                out=p.la[:, 0:512], in0=p.lns[0:64, 1536:2048],
                scalar1=p.expA[:], scalar2=None, op0=ALU.mult), "sV")

        if g < 16:
            for fi in range(2):
                fc = 2 * g + fi
                par_d = fc % 2
                p.wait("vector", "sA", evac_done[fc])
                hv = p.run("vector", lambda E, fc=fc: E.tensor_copy(
                    p.xrr[:, sl((fc % 6) * 515, 3)], p.hist[:, sl(fc * 3, 3)]), "sV")
                p.wait("tensor", "sV", hv)
                p.wait("tensor", *dgsem[fc])
                cps = p.psA[:, sl(1024 + par_d * 512, 512)]
                if fc >= 2:
                    p.wait("tensor", "sA", kb._conv_evac.get(par_d, 0))
                for j in range(4):
                    p.run("tensor", lambda E, j=j, par_d=par_d, fc=fc, cps=cps: E.matmul(
                        cps, p.dgbuf[:, sl(par_d * 512 + j * 128, 128)],
                        p.xrr[:, (fc % 6) * 515 + j:(fc % 6) * 515 + j + 512],
                        start=(j == 0), stop=(j == 3)), None)
                cv = p.run("tensor", lambda E: E.drain(), "sT")
                conv_done[fc] = cv
                dg_free[par_d] = cv
                if fc + 2 < 32:
                    dgsem[fc + 2] = stream_diag(fc + 2)
                p.wait("vector", "sV", hv)
                p.run("vector", lambda E, fc=fc: E.tensor_copy(
                    p.hist[:, sl(fc * 3, 3)], p.xrr[:, sl((fc % 6) * 515 + 3 + 509, 3)]), "sV")
                if fc >= 4:
                    p.wait("scalar", "sT", vtr_done[fc - 4])
                p.wait("scalar", "sT", cv)
                xe = p.run("scalar", lambda E, fc=fc, cps=cps: E.activation(
                    out=p.xcr[:, sl((fc % 4) * 512, 512)], in_=cps,
                    func=AF.Silu, bias=p.cb_cols[:, fc:fc + 1], scale=1.0), "sA")
                kb._conv_evac[par_d] = xe
                p.wait("tensor", "sA", xe)
                for c in range(NCH):
                    p.run("tensor", lambda E, fc=fc, c=c: E.transpose(
                        p.tr_ps[:, sl(c * 128, 128)],
                        p.xcr[:, sl((fc % 4) * 512 + c * 128, 128)], p.eye128_b[:]), None)
                tv = p.run("tensor", lambda E: E.drain(), "sT")
                vtr_done[fc] = tv
                p.wait("scalar", "sT", tv)
                ve = p.run("scalar", lambda E, fc=fc: E.activation(
                    out=p.vt[:].rearrange("q (c f) -> q c f", c=NCH)[:, :, sl(fc * 128, 128)],
                    in_=p.tr_ps[:].rearrange("q (c f) -> q c f", c=NCH),
                    func=AF.Copy, bias=0.0, scale=1.0), "sA")
                p.wait("tensor", "sA", ve)


def emit_inproj_z(kb: "KB", t):
    p = kb
    zfree = p.gfree.setdefault('zw', {})
    evs = {}

    def stream(zc, kg):
        par = (zc * 8 + kg) % 2
        if par in zfree:
            p.wait("sync", "sT", zfree[par])
        sname = "zwA" if par == 0 else "zwB"
        p.dma("sync", p.zwbuf[:, sl(par * 1024, 1024)].rearrange("q (k c) -> q k c", k=2),
              p.wsc_in[sl(kg * 2, 2), :, sl(zc * 512, 512)].rearrange("k q c -> q k c"), sem=sname)
        return (sname, p.v(sname))

    sems = {(0, 0): stream(0, 0), (0, 1): stream(0, 1)}
    for zc in range(8):
        for kg in range(8):
            par = (zc * 8 + kg) % 2
            p.wait("tensor", *sems[(zc, kg)])
            nxt = (zc, kg + 2) if kg + 2 < 8 else (zc + 1, kg - 6)
            for tsub in range(NCH):
                psv = p.psA[:, sl(tsub * 512, 512)]
                if kg == 0 and tsub in evs:
                    p.wait("tensor", "sA", evs[tsub])
                for k4 in range(2):
                    k = kg * 2 + k4
                    p.run("tensor", lambda E, k=k, k4=k4, kg=kg, par=par, tsub=tsub, psv=psv: E.matmul(
                        psv, p.normed()[:, sl(k * 512 + tsub * 128, 128)],
                        p.zwbuf[:, sl(par * 1024 + k4 * 512, 512)],
                        start=(kg == 0 and k4 == 0), stop=(kg == 7 and k4 == 1)), None)
            mv = p.run("tensor", lambda E: E.drain(), "sT")
            zfree[par] = mv
            if nxt[0] < 8:
                sems[nxt] = stream(*nxt)
        p.wait("scalar", "sT", p.v("sT"))
        p.wait("scalar", "sSO", p.v("sSO"))
        for tsub in range(NCH):
            evs[tsub] = p.run("scalar", lambda E, tsub=tsub: E.activation(
                out=p.prep_b[:, sl(tsub * 512, 512)], in_=p.psA[:, sl(tsub * 512, 512)],
                func=AF.Silu, bias=0.0, scale=1.0), "sA")
        p.wait("gpsimd", "sA", p.v("sA"))
        p.dma("gpsimd", p.zsc[:, :, sl(zc * 512, 512)].rearrange("c q f -> q c f"),
              p.prep_b[:, 0:2048].rearrange("q (c f) -> q c f", c=4), sem="sSO")
    kb._zsc_done = p.v("sSO")


def emit_ssd_chunk(kb: "KB", t, c):
    p = kb
    la_sl = p.la[:, sl(c * S, S)]
    csh_sl = p.csh[:, 0:S]
    cst_sl = p.cst[:, 0:NH]
    et_sl = p.et[:, 0:NH]
    rt_sl = p.rt[:, 0:NH]
    smt_sl = p.smt[:, 0:S]
    kcT = p.bm_b[:, sl(c * S, S)]
    qcT = p.cm_b[:, sl(c * S, S)]
    vtc = p.vt[:, sl(c * DI, DI)]

    if c == 0:
        p.wait("vector", "sSO", p.v("sSO"))
        p.wait("scalar", "sSO", kb._zsc_done)
        p.wait("scalar", "sV", kb._prep2_v)
        p.wait("scalar", "p2OA", kb._prep2_o)
        p.wait("scalar", "p2OB", getattr(kb, "_prep2_o2", 0))
        p.wait("gpsimd", "sV", kb._prep2_v)
    p.wait("gpsimd", "sV", kb._zbuf_free)
    p.wait("gpsimd", "sSO", kb._zsc_done)
    zv = p.dma("gpsimd", p.zbuf[:, 0:DI], p.zsc[c], sem="sZB")

    sv = p.run("vector", lambda E: E.tensor_tensor_scan(
        out=csh_sl, data0=la_sl, data1=la_sl, initial=0.0,
        op0=ALU.add, op1=ALU.bypass), "sV")
    p.wait("gpsimd", "sV", sv)
    p.wait("gpsimd", "sCB", kb._csflat_free)
    flv = p.dma("gpsimd", p.csflat[0], csh_sl, sem="sFL")
    p.wait("tensor", "sV", sv)
    trv = p.run("tensor", lambda E: E.transpose(
        p.ms_ps[:, 0:64], csh_sl, p.eye64_f[:]), "sT")
    p.wait("vector", "sT", trv)
    cstv = p.run("vector", lambda E: E.tensor_copy(cst_sl, p.ms_ps[:, 0:64]), "sV")
    p.wait("scalar", "sV", cstv)
    etv = p.run("scalar", lambda E: E.activation(
        out=et_sl, in_=cst_sl, func=AF.Exp, bias=0.0, scale=1.0), "sA")
    totcol = p.csh[:, S - 1:S]
    rhv = p.run("scalar", lambda E: E.activation(
        out=p.rh[:], in_=csh_sl, func=AF.Exp, bias=totcol, scale=-1.0), "sA")
    dcv = p.run("scalar", lambda E: E.activation(
        out=p.dcol[:], in_=totcol, func=AF.Exp, bias=0.0, scale=1.0), "sA")
    p.wait("tensor", "sA", rhv)
    rtv = p.run("tensor", lambda E: E.transpose(
        p.tr_ps[:, 0:64], p.rh[:], p.eye128_b[0:64, 0:64]), "sT")
    p.wait("vector", "sT", rtv)
    rtev = p.run("vector", lambda E: E.tensor_copy(rt_sl, p.tr_ps[:, 0:64]), "sV")
    p.wait("tensor", "sA", dcv)
    p.wait("tensor", "sV", cstv)
    dtv = p.run("tensor", lambda E: E.transpose(
        p.ms_ps[0:64, 64:128], p.dcol[:].broadcast_to([NH, 64]), p.eye64_f[:]), "sT")
    p.wait("vector", "sT", dtv)
    dtev = p.run("vector", lambda E: E.tensor_copy(p.dtile[:], p.ms_ps[0:64, 64:128]), "sV")
    p.wait("tensor", "sV", rtev)
    kctv = p.run("tensor", lambda E: E.transpose(
        p.tr_ps[:, 64:128], kcT, p.eye128_b[0:64, 0:64]), "sT")
    p.wait("vector", "sT", kctv)
    kcev = p.run("vector", lambda E: E.tensor_copy(p.kc_sb[:, 0:64], p.tr_ps[:, 64:128]), "sV")
    p.wait("tensor", "sV", dtev)
    scv = p.run("tensor", lambda E: E.matmul(
        p.ms_ps[:, 128:256], kcT, qcT, start=True, stop=True), "sT")
    p.wait("vector", "sT", scv)
    smv = p.run("vector", lambda E: E.tensor_tensor(
        out=smt_sl, in0=p.ms_ps[:, 128:256], in1=p.mask01_b[:], op=ALU.mult), "sV")
    yq_a = p.psA[:, 0:1024]
    yq_b = p.psA[:, 1024:2048]
    st_q = p.psB[0:64, 0:1024]
    last_c2 = None
    last_u2 = None
    dslot_free = {}
    csb_free = None
    last_rv_users = None
    csb_free = kb._csb_free
    for q in range(4):
        dslot = p.datt[:]
        p.selfsync("vector")
        p.wait("vector", "sV", rtev)
        if last_rv_users is not None:
            p.wait("vector", "sT", last_rv_users)
        rvv = p.run("vector", lambda E, q=q: E.tensor_tensor(
            out=p.rv[:].rearrange("z (h e) -> z h e", h=16),
            in0=vtc[:, sl(q * 1024, 1024)].rearrange("z (h e) -> z h e", h=16),
            in1=rt_sl[:, sl(q * 16, 16)].rearrange("z (h o) -> z h o", o=1).broadcast_to([128, 16, HD]),
            op=ALU.mult), "sV")
        vdv = p.run("vector", lambda E, q=q: E.tensor_tensor(
            out=p.vd[:].rearrange("z (h e) -> z h e", h=16),
            in0=vtc[:, sl(q * 1024, 1024)].rearrange("z (h e) -> z h e", h=16),
            in1=p.drep[:, sl(q * 16, 16)].rearrange("z (h o) -> z h o", o=1).broadcast_to([128, 16, HD]),
            op=ALU.mult), "sV")
        if csb_free:
            p.wait("gpsimd", "sV", csb_free)
        p.wait("gpsimd", "sFL", flv)
        cbv = p.dma("gpsimd", p.csb[:],
                    p.csflat[0].rearrange("h s -> (h s)")[sl(q * 2048, 2048)]
                    .rearrange("(a f) -> a f", a=1).broadcast_to([128, 2048]), sem="sCB")
        p.wait("vector", "sCB", cbv)
        dsv = p.run("vector", lambda E, q=q: E.tensor_tensor(
            out=p.darg[:].rearrange("z (h s) -> z h s", h=16),
            in0=p.csb[:].rearrange("z (h s) -> z h s", h=16),
            in1=cst_sl[:, sl(q * 16, 16)].rearrange("z (h o) -> z h o", o=1).broadcast_to([128, 16, S]),
            op=ALU.subtract), "sV")
        csb_free = dsv
        if dslot_free:
            p.wait("scalar", "sT", dslot_free[0])
        p.wait("scalar", "sV", dsv)
        absv = p.run("scalar", lambda E, dslot=dslot: E.activation(
            out=dslot, in_=p.darg[:], func=AF.Abs, bias=0.0, scale=1.0), "sA")
        p.wait("scalar", "sA", absv)
        expv = p.run("scalar", lambda E, dslot=dslot: E.activation(
            out=p.darg[:], in_=dslot, func=AF.Exp, bias=0.0, scale=-1.0), "sA")
        p.wait("vector", "sA", expv)
        p.wait("vector", "sV", smv)
        attv = p.run("vector", lambda E, dslot=dslot: E.tensor_tensor(
            out=dslot.rearrange("z (h s) -> z h s", h=16),
            in0=p.darg[:].rearrange("z (h s) -> z h s", h=16),
            in1=smt_sl.rearrange("z (o s) -> z o s", o=1).broadcast_to([128, 16, S]),
            op=ALU.mult), "sV")

        if last_c2 is not None:
            p.wait("tensor", "sV", last_c2)
        for j in range(2):
            p.run("tensor", lambda E, q=q, j=j: E.matmul(
                yq_b[:, sl(j * 512, 512)], qcT, p.h_bf[:, sl(q * 1024 + j * 512, 512)],
                start=True, stop=True), None)
        iv = p.run("tensor", lambda E: E.drain(), "sT")
        p.wait("tensor", "sV", attv)
        p.wait("tensor", "sV", vdv)
        for j in range(2):
            p.run("tensor", lambda E, q=q, j=j: E.matmul(
                yq_a[:, sl(j * 512, 512)], p.eye128_b[:], p.vd[:, sl(j * 512, 512)],
                start=True, stop=False), None)
            for i in range(8):
                hl = j * 8 + i
                p.run("tensor", lambda E, q=q, j=j, i=i, hl=hl, dslot=dslot: E.matmul(
                    yq_a[:, sl(j * 512 + i * 64, 64)], dslot[:, sl(hl * 128, 128)],
                    vtc[:, sl((q * 16 + hl) * 64, 64)],
                    start=False, stop=(i == 7)), None)
        av2 = p.run("tensor", lambda E: E.drain(), "sT")
        dslot_free[0] = av2
        p.wait("tensor", "sV", rvv)
        p.wait("tensor", "sV", kcev)
        if last_u2 is not None:
            p.wait("tensor", "sV", last_u2)
        for j in range(2):
            p.run("tensor", lambda E, q=q, j=j: E.matmul(
                st_q[:, sl(j * 512, 512)], p.kc_sb[:, 0:64], p.rv[:, sl(j * 512, 512)],
                start=True, stop=True), None)
        stv = p.run("tensor", lambda E: E.drain(), "sT")
        last_rv_users = stv

        p.wait("vector", "sT", av2)
        p.wait("vector", "sA", etv)
        c1 = p.run("vector", lambda E, q=q: E.tensor_tensor(
            out=p.y_tmp[:].rearrange("z (h e) -> z h e", h=16),
            in0=yq_b.rearrange("z (h e) -> z h e", h=16),
            in1=et_sl[:, sl(q * 16, 16)].rearrange("z (h o) -> z h o", o=1).broadcast_to([128, 16, HD]),
            op=ALU.mult), "sV")
        p.wait("vector", "sV", c1)
        c2 = p.run("vector", lambda E, q=q: E.tensor_tensor(
            out=p.y_sb[:, sl(q * 1024, 1024)], in0=p.y_tmp[:], in1=yq_a, op=ALU.add), "sV")
        last_c2 = c2
        p.wait("vector", "sT", stv)
        p.wait("vector", "sV", dtev)
        u1 = p.run("vector", lambda E, q=q: E.tensor_tensor(
            out=p.h_sb[:, sl(q * 1024, 1024)].rearrange("n (h e) -> n h e", h=16),
            in0=p.h_sb[:, sl(q * 1024, 1024)].rearrange("n (h e) -> n h e", h=16),
            in1=p.dtile[:, sl(q * 16, 16)].rearrange("n (h o) -> n h o", o=1).broadcast_to([NST, 16, HD]),
            op=ALU.mult), "sV")
        p.wait("vector", "sV", u1)
        u2 = p.run("vector", lambda E, q=q: E.tensor_tensor(
            out=p.h_sb[:, sl(q * 1024, 1024)], in0=p.h_sb[:, sl(q * 1024, 1024)],
            in1=st_q, op=ALU.add), "sV")
        last_u2 = u2
        p.wait("scalar", "sV", u2)
        p.wait("scalar", "sT", iv)
        p.run("scalar", lambda E, q=q: E.activation(
            out=p.h_bf[:, sl(q * 1024, 1024)], in_=p.h_sb[:, sl(q * 1024, 1024)],
            func=AF.Copy, bias=0.0, scale=1.0), "sA")

    kb._csb_free = csb_free
    kb._csflat_free = p.v("sCB")
    p.wait("scalar", "sV", last_c2)
    sqv = p.run("scalar", lambda E: E.activation(
        out=p.ygt[:], in_=p.y_sb[:], func=AF.Square, bias=0.0, scale=1.0,
        accum_out=p.ss_col[:]), "sA")
    p.wait("vector", "sA", sqv)
    r1 = p.run("vector", lambda E: E.tensor_scalar(
        out=p.ms_col[:], in0=p.ss_col[:], scalar1=1.0 / DI, scalar2=1e-6,
        op0=ALU.mult, op1=ALU.add), "sV")
    p.wait("vector", "sV", r1)
    r2 = p.run("vector", lambda E: E.reciprocal(p.ss_col[:], p.ms_col[:]), "sV")
    p.wait("scalar", "sV", r2)
    rsv = p.run("scalar", lambda E: E.activation(
        out=p.rstd2[:], in_=p.ss_col[:], func=AF.Sqrt, bias=0.0, scale=1.0), "sA")
    p.wait("vector", "sA", rsv)
    p.wait("vector", "sV", last_c2)
    p.wait("vector", "sZB", zv)
    gv = p.run("vector", lambda E: E.scalar_tensor_tensor(
        out=p.ygt[:], in0=p.y_sb[:], scalar=p.rstd2[:], in1=p.zbuf[:, 0:DI],
        op0=ALU.mult, op1=ALU.mult), "sV")
    kb._zbuf_free = gv
    p.wait("tensor", "sV", gv)
    lastev = None
    for fg in range(8):
        if lastev is not None:
            p.wait("tensor", "sA", lastev)
        for f4 in range(4):
            fc = fg * 4 + f4
            p.run("tensor", lambda E, fc=fc, f4=f4: E.transpose(
                p.tr_ps[:, sl(f4 * 128, 128)], p.ygt[:, sl(fc * 128, 128)], p.eye128_b[:]), None)
        tv = p.run("tensor", lambda E: E.drain(), "sT")
        p.wait("scalar", "sT", tv)
        dst = (p.slotA if fg < 4 else p.slotB)
        lastev = p.run("scalar", lambda E, dst=dst, fg=fg, c=c: E.activation(
            out=dst.rearrange("z (f x) -> z f x", f=16)[:, sl((fg % 4) * 4, 4), sl(c * 128, 128)],
            in_=p.tr_ps[:].rearrange("z (f x) -> z f x", f=4),
            func=AF.Copy, bias=0.0, scale=1.0), "sA")


def emit_outproj(kb: "KB", t):
    p = kb
    p.wait("gpsimd", "sT", p.v("sT"))
    p.wait("gpsimd", "sV", p.v("sV"))
    uld = p.dma("gpsimd", p.xslot[:].rearrange("z (a d_) -> z a d_", a=NCH),
                p.u[sl(t * TT, TT), :].rearrange("(a z) d_ -> z a d_", z=128), sem="sU")
    ofree = p.gfree.setdefault('ow', {})
    evs = {}

    def stream(cb, kg):
        par = (cb * 16 + kg) % 2
        if par in ofree:
            p.wait("sync", "sT", ofree[par])
        sname = "owA" if par == 0 else "owB"
        p.dma("sync", p.owbuf[:, sl(par * 1024, 1024)].rearrange("z (k c) -> z k c", k=2),
              p.wsc_out[sl(kg * 2, 2), :, sl(cb * 512, 512)].rearrange("k q c -> q k c"), sem=sname)
        return (sname, p.v(sname))

    sems = {(0, 0): stream(0, 0), (0, 1): stream(0, 1)}
    for cb in range(4):
        for kg in range(16):
            par = (cb * 16 + kg) % 2
            p.wait("tensor", *sems[(cb, kg)])
            nxt = (cb, kg + 2) if kg + 2 < 16 else (cb + 1, kg + 2 - 16)
            for tsub in range(NCH):
                psv = p.psA[:, sl(tsub * 512, 512)]
                if kg == 0 and tsub in evs:
                    p.wait("tensor", "sV", evs[tsub])
                for k4 in range(2):
                    k = kg * 2 + k4
                    p.run("tensor", lambda E, k=k, k4=k4, kg=kg, par=par, tsub=tsub, psv=psv: E.matmul(
                        psv, p.ygf_slice(k, tsub), p.owbuf[:, sl(par * 1024 + k4 * 512, 512)],
                        start=(kg == 0 and k4 == 0), stop=(kg == 15 and k4 == 1)), None)
            mv = p.run("tensor", lambda E: E.drain(), "sT")
            ofree[par] = mv
            if nxt[0] < 4:
                sems[nxt] = stream(*nxt)
        p.wait("vector", "sT", p.v("sT"))
        p.wait("vector", "sU", uld)
        for tsub in range(NCH):
            dst = p.xslot[:, sl(tsub * D + cb * 512, 512)]
            evs[tsub] = p.run("vector", lambda E, dst=dst, tsub=tsub: E.tensor_tensor(
                out=dst, in0=p.psA[:, sl(tsub * 512, 512)], in1=dst, op=ALU.add), "sV")


def emit_ffn(kb: "KB", t):
    p = kb
    n2f = p.slotA
    for tsub in range(NCH):
        p.selfsync("vector")
        p.selfsync("scalar")
        psv = p.psB[:, 0:128]
        for k in range(NK):
            p.run("tensor", lambda E, k=k, tsub=tsub, psv=psv: E.matmul(
                psv[:, 0:118], p.n2f_slice(k, tsub), p.wsmall_sb[:, sl(k * 118, 118)],
                start=(k == 0), stop=False), None)
        p.run("tensor", lambda E, psv=psv: E.matmul(
            psv[:, 0:118], p.ones_row[:], p.brow_sb[:], start=False, stop=True), None)
        smv = p.run("tensor", lambda E: E.drain(), "sT")
        p.wait("vector", "sT", smv)
        lg = psv[:, 16:22]
        v1 = p.run("vector", lambda E, lg=lg: E.reduce_max(
            out=p.m1c[:], in_=lg, axis=AX.X), "sV")
        p.wait("vector", "sV", v1)
        v2 = p.run("vector", lambda E: E.tensor_scalar(
            out=p.ms_col[:], in0=p.m1c[:], scalar1=-1.0, scalar2=None, op0=ALU.mult), "sV")
        p.wait("scalar", "sV", v2)
        a1 = p.run("scalar", lambda E, lg=lg: E.activation(
            out=p.eg[:], in_=lg, func=AF.Exp, bias=p.ms_col[:], scale=1.0,
            accum_out=p.sumg[:]), "sA")
        p.wait("vector", "sA", a1)
        v3 = p.run("vector", lambda E: E.reciprocal(p.rsg[:], p.sumg[:]), "sV")
        p.wait("vector", "sV", v3)
        v4 = p.run("vector", lambda E: E.tensor_scalar(
            out=p.gates[:], in0=p.eg[:], scalar1=p.rsg[:], scalar2=None, op0=ALU.mult), "sV")
        p.wait("vector", "sV", v4)
        v5 = p.run("vector", lambda E: E.reduce_max(
            out=p.m1c[:], in_=p.gates[:], axis=AX.X), "sV")
        p.wait("vector", "sV", v5)
        v6 = p.run("vector", lambda E: E.tensor_scalar(
            out=p.gm[:], in0=p.gates[:], scalar1=p.m1c[:], scalar2=-1e30,
            op0=ALU.is_ge, op1=ALU.mult), "sV")
        p.wait("vector", "sV", v6)
        v7 = p.run("vector", lambda E: E.tensor_tensor(
            out=p.gm[:], in0=p.gates[:], in1=p.gm[:], op=ALU.add), "sV")
        p.wait("vector", "sV", v7)
        v8 = p.run("vector", lambda E: E.reduce_max(
            out=p.m2c[:], in_=p.gm[:], axis=AX.X), "sV")
        p.wait("vector", "sV", v8)
        v9 = p.run("vector", lambda E: E.tensor_scalar(
            out=p.wpe[:], in0=p.gates[:], scalar1=p.m2c[:], scalar2=None, op0=ALU.is_ge), "sV")
        p.wait("vector", "sV", v9)
        v10 = p.run("vector", lambda E: E.tensor_tensor(
            out=p.wpe[:], in0=p.gates[:], in1=p.wpe[:], op=ALU.mult), "sV")
        p.wait("scalar", "sT", smv)
        a2 = p.run("scalar", lambda E, psv=psv: E.activation(
            out=p.ffn_small[:], in_=psv[:, 0:118], func=AF.Copy, bias=0.0, scale=1.0), "sA")
        p.wait("vector", "sA", a2)
        p.wait("vector", "sV", v10)
        s1 = p.run("vector", lambda E: E.tensor_tensor(
            out=p.stackt[:, 16:112].rearrange("z (e r) -> z e r", e=6),
            in0=p.ffn_small[:, 22:118].rearrange("z (e r) -> z e r", e=6),
            in1=p.wpe[:].rearrange("z (e o) -> z e o", o=1).broadcast_to([128, 6, 16]),
            op=ALU.mult), "sV")
        s2 = p.run("vector", lambda E: E.tensor_copy(
            p.stackt[:, 0:16], p.ffn_small[:, 0:16]), "sV")
        p.wait("tensor", "sV", s2)
        tvv = p.run("tensor", lambda E: E.transpose(
            p.tr_ps[0:113, 0:128], p.stackt[:], p.eye128_b[:]), "sT")
        p.wait("vector", "sT", tvv)
        sev = p.run("vector", lambda E, tsub=tsub: E.tensor_copy(
            p.stackT[:, sl(tsub * 128, 128)], p.tr_ps[0:113, 0:128]), "sV")
        p.wait("tensor", "sV", sev)

    bfree = p.gfree.setdefault('bw', {})
    evs = {}
    stv = {}

    def stream(cb, kg):
        par = (cb * 8 + kg) % 2
        if par in bfree:
            p.wait("sync", "sT", bfree[par])
        sname = "bwA" if par == 0 else "bwB"
        p.dma("sync", p.bwbuf[:, sl(par * 1024, 1024)].rearrange("z (k c) -> z k c", k=2),
              p.wsc_base[sl(kg * 2, 2), :, sl(cb * 512, 512)].rearrange("k q c -> q k c"), sem=sname)
        return (sname, p.v(sname))

    sems = {(0, 0): stream(0, 0), (0, 1): stream(0, 1)}
    for cb in range(4):
        for kg in range(8):
            par = (cb * 8 + kg) % 2
            p.wait("tensor", *sems[(cb, kg)])
            nxt = (cb, kg + 2) if kg + 2 < 8 else (cb + 1, kg + 2 - 8)
            for tsub in range(NCH):
                psv = p.psA[:, sl(tsub * 512, 512)]
                if kg == 0 and tsub in evs:
                    p.wait("tensor", "sV", evs[tsub])
                for k4 in range(2):
                    k = kg * 2 + k4
                    p.run("tensor", lambda E, k=k, k4=k4, kg=kg, par=par, tsub=tsub, psv=psv: E.matmul(
                        psv, p.n2f_slice(k, tsub), p.bwbuf[:, sl(par * 1024 + k4 * 512, 512)],
                        start=(kg == 0 and k4 == 0), stop=False), None)
            mv = p.run("tensor", lambda E: E.drain(), "sT")
            bfree[par] = mv
            if nxt[0] < 4:
                sems[nxt] = stream(*nxt)
        for tsub in range(NCH):
            psv = p.psA[:, sl(tsub * 512, 512)]
            p.run("tensor", lambda E, tsub=tsub, psv=psv, cb=cb: E.matmul(
                psv, p.stackT[:, sl(tsub * 128, 128)], p.bdown_sb[:, sl(cb * 512, 512)],
                start=False, stop=True), None)
        dv = p.run("tensor", lambda E: E.drain(), "sT")
        p.wait("vector", "sT", dv)
        if cb > 0:
            p.wait("vector", "sSO", stv[cb - 1])
        for tsub in range(NCH):
            evs[tsub] = p.run("vector", lambda E, tsub=tsub, cb=cb: E.tensor_tensor(
                out=p.x2_sb[:, sl(tsub * 512, 512)], in0=p.psA[:, sl(tsub * 512, 512)],
                in1=p.xslot[:, sl(tsub * D + cb * 512, 512)], op=ALU.add), "sV")
        p.wait("gpsimd", "sV", max(evs.values()))
        for tsub in range(NCH):
            p.dma("gpsimd",
                  p.out[sl(t * TT + tsub * 128, 128), sl(cb * 512, 512)],
                  p.x2_sb[:, sl(tsub * 512, 512)], sem="sSO")
        stv[cb] = p.v("sSO")


def build_full(ntiles=NTILE, debug=()):
    kb = KB(debug=debug)
    p = kb
    emit_prep(kb)
    for t in range(ntiles):
        p.wait("gpsimd", "sV", p.v("sV"))
        p.wait("gpsimd", "sA", p.v("sA"))
        p.wait("gpsimd", "sT", p.v("sT"))
        uld = p.dma("gpsimd", p.xslot[:].rearrange("z (a d_) -> z a d_", a=NCH),
                    p.u[sl(t * TT, TT), :].rearrange("(a z) d_ -> z a d_", z=128), sem="sU")
        p.wait("vector", "sU", uld)
        p.wait("scalar", "sU", uld)
        emit_ln(kb, t, kb.slotA, "ln1")
        p.barrier()
        kb.tap("normed_t%d" % t, kb.slotA[:])
        emit_inproj_f(kb, t)
        p.barrier()
        kb.tap("vt_t%d" % t, kb.vt[:])
        kb.tap("bm_t%d" % t, kb.bm_b[:])
        kb.tap("cm_t%d" % t, kb.cm_b[:])
        kb.tap("la_t%d" % t, kb.la[:])
        emit_inproj_z(kb, t)
        if t == 0:
            emit_prep_outbase(kb)
        p.barrier()
        for c in range(NCH):
            emit_ssd_chunk(kb, t, c)
            p.barrier()
            kb.tap("ysb_t%d_c%d" % (t, c), kb.y_sb[:])
            kb.tap("ygt_t%d_c%d" % (t, c), kb.ygt[:])
        emit_outproj(kb, t)
        p.barrier()
        kb.tap("x1_t%d" % t, kb.xslot[:])
        emit_ln(kb, t, kb.slotA, "ln2")
        p.barrier()
        emit_ffn(kb, t)
        p.barrier()
    return kb


def _make_in_map(inputs, core):
    f = np.ascontiguousarray
    u = f(np.asarray(inputs["L0_attn_u"][core], np.float32))
    w_small = np.concatenate([
        np.asarray(inputs["L0_ffn_A_s"], np.float32),
        np.asarray(inputs["L0_ffn_W_router"], np.float32),
        np.asarray(inputs["L0_ffn_A_r"], np.float32).transpose(1, 0, 2).reshape(2048, 96),
    ], axis=1)
    b_down = np.concatenate([
        np.asarray(inputs["L0_ffn_B_s"], np.float32),
        np.asarray(inputs["L0_ffn_B_r"], np.float32).reshape(96, 2048),
    ], axis=0)
    brow = np.zeros((1, 118), np.float32)
    brow[0, 16:22] = np.asarray(inputs["L0_ffn_b_router"], np.float32)
    mask01 = np.triu(np.ones((128, 128), np.float32))
    return dict(
        u=u,
        w_in=f(np.asarray(inputs["L0_attn_in_proj_w"], np.float32)),
        convw=f(np.asarray(inputs["L0_attn_conv_weight"], np.float32)
                .reshape(4, 32, 128).transpose(2, 1, 0).reshape(128, 128)),
        convb=f(np.asarray(inputs["L0_attn_conv_bias"], np.float32).reshape(32, 128).T),
        A_log=np.asarray(inputs["L0_attn_A_log"], np.float32).reshape(-1, 1),
        dt_bias=np.asarray(inputs["L0_attn_dt_bias"], np.float32).reshape(-1, 1),
        D_param=np.asarray(inputs["L0_attn_D_param"], np.float32).reshape(1, -1),
        norm_scale=f(np.asarray(inputs["L0_attn_norm_scale"], np.float32).reshape(32, 128).T),
        w_out=f(np.asarray(inputs["L0_attn_out_proj_w"], np.float32)),
        w_base=f(np.asarray(inputs["L0_ffn_W_base"], np.float32)),
        b_base=np.asarray(inputs["L0_ffn_b_base"], np.float32).reshape(1, -1),
        w_small=f(w_small),
        b_down=f(b_down),
        brow=brow,
        eye128=np.eye(128, dtype=np.float32),
        mask01=mask01,
    )


def kernel(**inputs):
    from concourse.bass_utils import run_bass_kernel_spmd
    kb = build_full(ntiles=NTILE)
    nc = kb.finalize()
    in_maps = [_make_in_map(inputs, c) for c in range(8)]
    res = run_bass_kernel_spmd(nc, in_maps, core_ids=list(range(8)))
    out = np.stack([np.asarray(r["out"], np.float32) for r in res.results], axis=0)
    return out
